# version 16
# baseline (speedup 1.0000x reference)
import numpy as np
import concourse.bass as bass
import concourse.mybir as mybir
from concourse.bass_utils import run_bass_kernel_spmd
from contextlib import ExitStack

F32 = mybir.dt.float32
BF16 = mybir.dt.bfloat16
AF = mybir.ActivationFunctionType
ALU = mybir.AluOpType
AX = mybir.AxisListType

D = 2048
F_REAL = 5461
FP = 5504
NCT = 16
NHT = 22
NFT = 43
T = 1024
W = 64
TL = 1089
TS_ = 1088
EPS = 1e-5
CH_X = [(0, 512), (512, 1024), (1024, 1089)]
CH_S = [(0, 512), (512, 1024), (1024, 1088)]
DEBUG = False

ENGINES = ("sp", "pe", "dve", "act", "pool")


N_LANES = {"sp": 6, "pool": 4, "act": 2, "dve": 2, "pe": 2}


class Builder:

    def __init__(self):
        self.groups = {e: [] for e in ENGINES}
        self.counts = {}
        self.lane_rr = {}
        self.lane_last = {}

    def g(self, engine, fn, deps=(), dma=False, settle=0):
        deps = [t for t in deps if t is not None]
        if dma:
            lane = self.lane_rr.get(engine, 0)
            self.lane_rr[engine] = (lane + 1) % N_LANES[engine]
            sem = f"d_{engine}{lane}"
            amt = 16
            prev = self.lane_last.get(sem)
            if prev is not None:
                deps.append(prev)
        else:
            sem = engine
            amt = 1
        v = self.counts.get(sem, 0) + amt
        self.counts[sem] = v
        self.groups[engine].append((fn, tuple(deps), sem, amt, settle, v))
        tok = (sem, v)
        if dma:
            self.lane_last[sem] = tok
        return tok

    def seq(self, engine, fns, deps=()):
        tok = None
        for i, f in enumerate(fns):
            tok = self.g(engine, f, deps=(list(deps) + [tok]) if i == 0 else [tok])
        return tok


def build_program():
    nc = bass.Bass("TRN2", target_bir_lowering=False)

    xsh = nc.dram_tensor("xsh", [D, TL], F32, kind="ExternalInput").ap()
    vecs = nc.dram_tensor("vecs", [128, 152], F32, kind="ExternalInput").ap()
    wk_h = nc.dram_tensor("wk_h", [NHT, 128, 2048], F32, kind="ExternalInput").ap()
    wr_h = nc.dram_tensor("wr_h", [NHT, 128, 2048], F32, kind="ExternalInput").ap()
    wv_h = nc.dram_tensor("wv_h", [NCT, 128, 2816], F32, kind="ExternalInput").ap()
    w1_h = nc.dram_tensor("w1_h", [NFT, 128, 2048], F32, kind="ExternalInput").ap()
    w3_h = nc.dram_tensor("w3_h", [NFT, 128, 2048], F32, kind="ExternalInput").ap()
    w2_h = nc.dram_tensor("w2_h", [NCT, 128, 5504], F32, kind="ExternalInput").ap()
    gbrow = nc.dram_tensor("gbrow", [3, 2048], F32, kind="ExternalInput").ap()
    yout = nc.dram_tensor("yout", [D, T], F32, kind="ExternalOutput").ap()
    dbg = {}
    if DEBUG:
        dbg["d_sr"] = nc.dram_tensor("d_sr", [NHT * 128, T], BF16, kind="ExternalOutput").ap()
        dbg["d_y"] = nc.dram_tensor("d_y", [D, T], F32, kind="ExternalOutput").ap()
        dbg["d_x2n"] = nc.dram_tensor("d_x2n", [D, T], BF16, kind="ExternalOutput").ap()
        dbg["d_x2"] = nc.dram_tensor("d_x2", [D, T], F32, kind="ExternalOutput").ap()
        dbg["d_g"] = nc.dram_tensor("d_g", [NFT * 128, T], BF16, kind="ExternalOutput").ap()
        dbg["d_acc"] = nc.dram_tensor("d_acc", [128, 200], F32, kind="ExternalOutput").ap()
        dbg["d_scl"] = nc.dram_tensor("d_scl", [128, 8], F32, kind="ExternalOutput").ap()
        dbg["d_rw"] = nc.dram_tensor("d_rw", [1, 1600], F32, kind="ExternalOutput").ap()

    B = Builder()

    with ExitStack() as st:
        sb = lambda n, s, d=F32: st.enter_context(nc.sbuf_tensor(n, s, d))
        A1 = sb("A1", [128, 22016])
        A2 = sb("A2", [128, 8192])
        A4 = sb("A4", [128, 11264])
        A3 = sb("A3", [128, 3072])
        BC = sb("BC", [128, 2178])
        STG = sb("STG", [128, 1536])
        RW = sb("RW", [1, 1600])
        VEC = sb("VEC", [128, 152])
        WDEC = sb("WDEC", [128, 22])
        DTMP = sb("DTMP", [128, 22])
        G4 = sb("G4", [128, 64], BF16)
        G2 = sb("G2", [128, 32], BF16)
        GB = sb("GB", [128, 80], BF16)
        ONE_R = sb("ONE_R", [1, 128])
        ONE_C = sb("ONE_C", [128, 1])
        ACC = sb("ACC", [128, 200])
        SCL = sb("SCL", [128, 8])
        X1R = sb("X1R", [128, 2180], BF16)
        GROWB = sb("GROWB", [128, 2048], BF16)

        PS = [st.enter_context(nc.psum_tensor(f"PS{i}", [128, 512], F32)) for i in range(8)]

        sems = {}
        for e in ENGINES:
            sems[e] = st.enter_context(nc.semaphore("s_" + e))
            for ln in range(N_LANES[e]):
                sems[f"d_{e}{ln}"] = st.enter_context(nc.semaphore(f"sd_{e}{ln}"))

        def cv(arena, c0, n):
            return arena[:, c0:c0 + n]

        def cb(arena, c0, nbf):
            return arena[:, c0:c0 + (nbf + 1) // 2].bitcast(BF16)[:, :nbf]

        xt = lambda ct: cv(A1, 1089 * ct, 1089)
        xm_base = lambda ct: 1089 * ct
        xmk = lambda ct: cb(A1, xm_base(ct), 1088)
        xmr = lambda ct: cb(A1, xm_base(ct) + 544, 1088)
        d_rot = [cb(A1, 18512 + 544 * i, 1088) for i in range(2)]
        a_sb = [cv(A1, 19600 + 1088 * i, 1088) for i in range(2)]
        aps_ph1 = cv(A1, 19600, 512)
        y_t = lambda ct: cv(A1, 1024 * ct, 1024)
        g_t = lambda ft: cb(A1, 512 * ft, 1024)
        aps1 = cv(A1, 18944, 512)
        aps2 = cv(A1, 19456, 512)
        aps3 = cv(A1, 19968, 512)
        xres3 = [cv(A1, 20992 + 512 * i, 512) for i in range(2)]

        wabs = [cv(A2, 2048 * i, 2048) for i in range(2)]
        state_t = cv(A2, 4096, 1088)
        x2n_t = lambda ct: cb(A2, 512 * ct, 1024)
        y2bf_t = lambda ct: cb(A2, 512 * ct, 1024)

        sr_t = lambda ht: cb(A4, 512 * ht, 1024)
        wstg1 = [cv(A4, 1024 * i, 1024) for i in range(2)]
        wstg3 = [cv(A4, 2048 + 1024 * i, 1024) for i in range(2)]
        qtmp1 = cv(A4, 4096, 1024)
        qtmp3 = cv(A4, 5120, 1024)
        q1b_t = lambda i: cb(A4, 6144 + 1024 * i, 2048)
        q3b_t = lambda i: cb(A4, 8192 + 1024 * i, 2048)
        p1w2 = cv(A4, 10240, 1024)
        q2b_t = lambda i: cb(A4, 2752 * i, 5504)
        wstg2 = [cv(A4, 5504 + 1024 * i, 1024) for i in range(2)]
        qtmp2 = cv(A4, 7552, 1024)
        xre6 = [cv(A4, 8576 + 512 * i, 512) for i in range(3)]
        out6 = [cv(A4, 10112 + 512 * i, 512) for i in range(2)]
        aps6 = cv(A4, 7552, 512)

        wkb = lambda i: cb(A3, 1024 * i, 2048)
        wrb = cb(A3, 2048, 2048)
        wvb = lambda i: cb(A3, 1408 * i, 2816)

        aCh = [cv(BC, o, n) for (o, n) in [(0, 512), (512, 512), (1024, 65)]]
        bCh = [cv(BC, o, n) for (o, n) in [(1089, 512), (1601, 512), (2113, 65)]]
        RWB = BC[0:1, 1024:1792].bitcast(BF16)
        RWB3 = BC[0:3, 1792:2048].bitcast(BF16)
        bcA, bcB, bcC, bcE = cv(BC, 0, 512), cv(BC, 512, 512), cv(BC, 1024, 512), cv(BC, 1536, 512)

        ybf_r = [cb(STG, 256 * i, 512) for i in range(2)]
        ysq_r = [cb(STG, 512 + 256 * i, 512) for i in range(2)]
        cb1 = ybf_r
        sb1 = ysq_r
        x2bf_r = [cb(STG, 1024 + 256 * i, 512) for i in range(2)]
        x2sq_r = [cb(STG, 256 * i, 512) for i in range(2)]
        silu_t = lambda i: cb(STG, 512 * i, 1024)
        y2sq_r = [cb(STG, 512 + 256 * i, 512) for i in range(2)]

        R_S = {k: RW[0:1, i:i + 1] for i, k in enumerate(
            ["S1", "S2", "S3", "S4", "S5", "eps", "eps5", "s1", "s3", "s2",
             "t1a", "t2a", "t1b", "t2b", "t1c", "t2c", "t1e", "t2e"])}
        ROWN = ["mu", "v1", "v2", "rstd", "brow", "ry"]
        ROW = {k: RW[0:1, 32 + 256 * i:32 + 256 * (i + 1)] for i, k in enumerate(ROWN)}
        row = lambda k, n: RW[0:1, 32 + 256 * ROWN.index(k):32 + 256 * ROWN.index(k) + n]

        tmk_c = lambda ct: VEC[:, ct:ct + 1]
        tmr_c = lambda ct: VEC[:, 16 + ct:17 + ct]
        gam_c = lambda ct: VEC[:, 32 + ct:33 + ct]
        bet_c = lambda ct: VEC[:, 48 + ct:49 + ct]
        dec_all = VEC[:, 64:86]
        mask_t = VEC[:, 86:150]
        thr1c, nthr1c = SCL[:, 0:1], SCL[:, 1:2]
        thr3c, nthr3c = SCL[:, 2:3], SCL[:, 3:4]
        thr2c, nthr2c = SCL[:, 4:5], SCL[:, 5:6]
        s1col = SCL[:, 6:7]

        t_vec = B.g("sp", lambda e: e.dma_start(VEC[:], vecs), dma=True)
        t_dec1 = B.g("act", lambda e: e.activation(DTMP[:], dec_all, AF.Exp), deps=[t_vec])
        t_wdec = B.g("act", lambda e: e.activation(WDEC[:], DTMP[:], AF.Exp, scale=-1.0),
                     deps=[t_dec1])

        def f_cv1(e):
            e.memset(ACC[:], 0.0)
            e.memset(SCL[:], 0.0)
            e.memset(RW[:], 0.0)
            e.memset(ONE_R[:], 1.0)
            e.memset(ONE_C[:], 1.0)
            e.memset(G4[:], 1.0)
            e.memset(G2[:], 1.0)
            return e.memset(R_S["eps"], EPS)
        def f_cv2(e):
            e.tensor_copy(G4[:, 1:64:4], VEC[:, 32:48])
            e.tensor_tensor(G4[:, 3:64:4], VEC[:, 32:48], VEC[:, 48:64], ALU.mult)
            e.tensor_copy(GB[:, 0:16], VEC[:, 32:48])
            e.tensor_copy(GB[:, 16:32], VEC[:, 48:64])
            return e.tensor_tensor(GB[:, 48:64], VEC[:, 32:48], VEC[:, 48:64], ALU.mult)
        t_constv = B.seq("dve", [f_cv1, f_cv2], deps=[t_vec])

        def f_const_a(e):
            e.activation(G4[:, 2:64:4], VEC[:, 32:48], AF.Square)
            e.activation(G2[:, 1:32:2], VEC[:, 32:48], AF.Square)
            e.activation(GB[:, 32:48], VEC[:, 32:48], AF.Square)
            return e.activation(GB[:, 64:80], VEC[:, 48:64], AF.Square)
        t_consta = B.g("act", f_const_a, deps=[t_vec, t_constv])

        t_smm = B.g("pe", lambda e: e.matmul(PS[7][0:1, 0:80], G4[:, 0:1], GB[:],
                                             start=True, stop=True),
                    deps=[t_consta, t_constv])

        tg0 = B.g("pool", lambda e: e.dma_start(GROWB[0:1, :], gbrow[0:1, :]), dma=True)
        tg1 = B.g("pool", lambda e: e.dma_start(GROWB[1:2, :], gbrow[1:2, :]), dma=True)
        tg2 = B.g("pool", lambda e: e.dma_start(GROWB[2:3, :], gbrow[2:3, :]), dma=True)
        t_growb_all = [tg0, tg1, tg2]

        def f_sred(e):
            for i, k in enumerate(["S1", "S2", "S3", "S4", "S5"]):
                last = e.tensor_reduce(R_S[k], PS[7][0:1, 16 * i:16 * (i + 1)], AX.X, ALU.add)
            return last
        t_sred = B.g("dve", f_sred, deps=[t_smm])

        p1_tiles = []
        for m, wh in ((1, w1_h), (3, w3_h)):
            for ft in range(NFT):
                p1_tiles.append((wh[ft], 2048, "w13"))
        n_p1_w1 = NFT
        n_p1_w13 = NFT * 2
        for ct in range(NCT):
            for h_ in range(6):
                lo = 1024 * h_
                hi = min(lo + 1024, 5504)
                p1_tiles.append((w2_h[ct, :, lo:hi], hi - lo, "w2"))
        n_p1_all = len(p1_tiles)
        p1 = {"i": 0, "tok": [None, None], "w2tok": None,
              "w1_done": None, "w13_done": None, "all_done": None}

        def emit_p1(nslices, allow_w2=False, extra_dep=None):
            for _ in range(nslices):
                i = p1["i"]
                if i >= n_p1_all:
                    return
                src, ncols, kind = p1_tiles[i]
                if kind == "w2" and not allow_w2:
                    return
                if kind == "w13":
                    r = i % 2
                    buf = wabs[r]
                    war = p1["tok"][r]
                else:
                    r = 0
                    buf = p1w2
                    war = p1["w2tok"]
                td = B.g("sp", lambda e, src=src, buf=buf, ncols=ncols: e.dma_start(
                    buf[:, :ncols], src), deps=[war, extra_dep], dma=True)
                ta = B.g("act", lambda e, buf=buf, i=i, ncols=ncols: e.activation(
                    buf[:, :ncols], buf[:, :ncols], AF.Abs, accum_out=ACC[:, i:i + 1]),
                    deps=[td])
                if kind == "w13":
                    p1["tok"][r] = ta
                else:
                    p1["w2tok"] = ta
                p1["i"] = i + 1
                if i + 1 == n_p1_w1:
                    p1["w1_done"] = ta
                if i + 1 == n_p1_w13:
                    p1["w13_done"] = ta
                if i + 1 == n_p1_all:
                    p1["all_done"] = ta

        t_x = [B.g("sp", lambda e, ct=ct: e.dma_start(xt(ct), xsh[128 * ct:128 * (ct + 1), :]),
                   dma=True) for ct in range(NCT)]

        st1_banks = {0: (PS[0], PS[3]), 1: (PS[1], PS[4]), 2: (PS[2], PS[5])}
        t_st1 = {}
        rotor_rd = {}
        for ci, (c0, c1) in enumerate(CH_X):
            n = c1 - c0
            for ct in range(NCT):
                r = ct % 2
                def f_c(e, ct=ct, r=r, c0=c0, c1=c1, n=n):
                    e.copy(cb1[r][:, :n], xt(ct)[:, c0:c1])
                    return e.activation(sb1[r][:, :n], xt(ct)[:, c0:c1], AF.Square)
                tc_ = B.g("act", f_c, deps=[t_x[ct], rotor_rd.get(r)])
                bA, bB = st1_banks[ci]
                def f_s(e, ct=ct, r=r, n=n, bA=bA, bB=bB):
                    e.matmul(bA[0:1, 0:n], G4[:, 0:1], cb1[r][:, :n],
                             start=(ct == 0), stop=(ct == NCT - 1))
                    return e.matmul(bB[0:1, 0:n], G4[:, 0:1], sb1[r][:, :n],
                                    start=(ct == 0), stop=(ct == NCT - 1))
                t_st1[(ci, ct)] = B.g("pe", f_s, deps=[tc_, t_consta])
                rotor_rd[r] = t_st1[(ci, ct)]
            emit_p1(2)

        def ln_rows(bA, bB, n_chunk, dep, eps_ap, bc_banks, war_bc):
            toks = []
            prev_bc = war_bc
            for sc in range(0, n_chunk, 256):
                nn = min(256, n_chunk - sc)
                def f_rv(e, sc=sc, nn=nn, bA=bA, bB=bB):
                    e.tensor_scalar(row("mu", nn), bA[0:1, sc:sc + nn], 1.0 / D, None,
                                    ALU.mult)
                    e.tensor_scalar(row("v1", nn), bB[0:1, sc:sc + nn], 1.0 / D, None,
                                    ALU.mult)
                    e.tensor_tensor(row("v2", nn), row("mu", nn), row("mu", nn), ALU.mult)
                    return e.tensor_tensor(row("v1", nn), row("v1", nn), row("v2", nn),
                                           ALU.subtract)
                trv = B.g("dve", f_rv, deps=[dep, prev_bc])
                te_ = B.g("act", lambda e, nn=nn, eps_ap=eps_ap: (
                    e.activation(row("v2", nn), row("v1", nn), AF.Ln, bias=eps_ap),
                    e.activation(row("rstd", nn), row("v2", nn), AF.Exp, scale=-0.5))[-1],
                    deps=[trv])
                tb_ = B.g("dve", lambda e, nn=nn: e.scalar_tensor_tensor(
                    row("brow", nn), row("mu", nn), -1.0, row("rstd", nn), ALU.mult, ALU.mult),
                    deps=[te_])
                def f_bc(e, sc=sc, nn=nn):
                    e.matmul(bc_banks[0][:, sc:sc + nn], ONE_R[:], row("rstd", nn),
                             start=True, stop=True)
                    return e.matmul(bc_banks[1][:, sc:sc + nn], ONE_R[:], row("brow", nn),
                                    start=True, stop=True)
                tbc = B.g("pe", f_bc, deps=[tb_])
                prev_bc = tbc
                toks.append(tbc)
            return toks[-1]

        t_bce1 = {}
        prev_bce = None
        for ci, (c0, c1) in enumerate(CH_X):
            n = c1 - c0
            bA, bB = st1_banks[ci]
            tbc = ln_rows(bA, bB, n, t_st1[(ci, NCT - 1)], R_S["eps"], (PS[6], PS[7]), prev_bce)
            def f_bce(e, ci=ci, n=n):
                e.tensor_copy(aCh[ci][:, :n], PS[6][:, 0:n])
                return e.tensor_copy(bCh[ci][:, :n], PS[7][:, 0:n])
            t_bce1[ci] = B.g("dve", f_bce, deps=[tbc])
            prev_bce = t_bce1[ci]

        t_mix = {}
        t_a1_prev = {"pool": None, "dve": None}
        POOL_CTS = {0, 2, 4, 6, 8, 10, 12}
        for ct in range(NCT):
            eng1 = "pool" if ct in POOL_CTS else "dve"
            ei = 0 if eng1 == "pool" else 1
            x1r = X1R[:, 1090 * ei:1090 * ei + 1089]
            sbase = 19600 + 512 * ei
            def f_am(e, ct=ct, x1r=x1r, sbase=sbase, ei=ei, eng1=eng1):
                sc_ap = cv(A1, sbase, 512)
                for ci, (c0, c1) in enumerate(CH_X):
                    n = c1 - c0
                    e.tensor_tensor(sc_ap[:, :n], xt(ct)[:, c0:c1], aCh[ci][:, :n], ALU.mult)
                    e.tensor_tensor(x1r[:, c0:c1], sc_ap[:, :n], bCh[ci][:, :n], ALU.add)
                dd = d_rot[ei]
                e.tensor_tensor(dd[:], x1r[:, 0:1088], x1r[:, 1:1089], ALU.subtract)
                if eng1 == "dve":
                    e.scalar_tensor_tensor(xmk(ct)[:], dd[:], tmk_c(ct), x1r[:, 1:1089],
                                           ALU.mult, ALU.add)
                    return e.scalar_tensor_tensor(xmr(ct)[:], dd[:], tmr_c(ct),
                                                  x1r[:, 1:1089], ALU.mult, ALU.add)
                e.tensor_tensor(xmk(ct)[:], dd[:], tmk_c(ct).broadcast_to([128, TS_]),
                                ALU.mult)
                e.tensor_tensor(xmk(ct)[:], xmk(ct)[:], x1r[:, 1:1089], ALU.add)
                e.tensor_tensor(xmr(ct)[:], dd[:], tmr_c(ct).broadcast_to([128, TS_]),
                                ALU.mult)
                return e.tensor_tensor(xmr(ct)[:], xmr(ct)[:], x1r[:, 1:1089], ALU.add)
            t_mix[ct] = B.g(eng1, f_am, deps=[t_bce1[2], t_x[ct], t_a1_prev[eng1]])
            t_a1_prev[eng1] = t_mix[ct]
            if ct % 2 == 1:
                emit_p1(2)
        t_mix_all = [t_a1_prev["pool"], t_a1_prev["dve"]]

        kb = {0: PS[0], 1: PS[1], 2: PS[2]}
        rb = {0: PS[3], 1: PS[4], 2: PS[5]}
        t_wk, t_wr, t_kmm, t_rmm, t_relu, t_sblk = {}, {}, {}, {}, {}, {}

        def emit_kmm(ht):
            r = ht % 2
            def f_k(e, r=r):
                for ci, (cs0, cs1) in enumerate(CH_S):
                    ns = cs1 - cs0
                    for ct in range(NCT):
                        last = e.matmul(kb[ci][:, 0:ns], wkb(r)[:, 128 * ct:128 * (ct + 1)],
                                        xmk(ct)[:, cs0:cs1], start=(ct == 0),
                                        stop=(ct == NCT - 1))
                return last
            t_kmm[ht] = B.g("pe", f_k, deps=[t_wk[ht], t_relu.get(ht - 1)] + t_mix_all)

        def emit_rmm(ht):
            def f_r(e):
                for ci, (cs0, cs1) in enumerate(CH_S):
                    ns = cs1 - cs0
                    for ct in range(NCT):
                        last = e.matmul(rb[ci][:, 0:ns], wrb[:, 128 * ct:128 * (ct + 1)],
                                        xmr(ct)[:, cs0:cs1], start=(ct == 0),
                                        stop=(ct == NCT - 1))
                return last
            t_rmm[ht] = B.g("pe", f_r, deps=[t_wr[ht], t_sblk.get(ht - 1)] + t_mix_all)
            r = ht % 2
            t_relu[ht] = B.g("act", lambda e, r=r: (
                e.activation(a_sb[r][:, 0:512], kb[0][:, 0:512], AF.Relu),
                e.activation(a_sb[r][:, 512:1024], kb[1][:, 0:512], AF.Relu),
                e.activation(a_sb[r][:, 1024:1088], kb[2][:, 0:64], AF.Relu))[-1],
                deps=[t_kmm[ht], t_sblk.get(ht - 2)])
            def f_sb(e, ht=ht, r=r):
                e.tensor_tensor(a_sb[r][:, 0:64], a_sb[r][:, 0:64], mask_t, ALU.mult)
                e.tensor_tensor_scan(state_t[:], WDEC[:, ht:ht + 1].broadcast_to([128, TS_]),
                                     a_sb[r][:], 0.0, ALU.mult, ALU.add)
                e.tensor_tensor(sr_t(ht)[:, 0:448], state_t[:, 64:512], rb[0][:, 64:512],
                                ALU.mult)
                e.tensor_tensor(sr_t(ht)[:, 448:960], state_t[:, 512:1024], rb[1][:, 0:512],
                                ALU.mult)
                return e.tensor_tensor(sr_t(ht)[:, 960:1024], state_t[:, 1024:1088],
                                       rb[2][:, 0:64], ALU.mult)
            t_sblk[ht] = B.g("dve", f_sb, deps=[t_relu[ht], t_wdec, t_rmm[ht],
                                                t_sblk.get(ht - 1)])

        for ht in range(NHT):
            r = ht % 2
            t_wk[ht] = B.g("pool", lambda e, ht=ht, r=r: e.dma_start(wkb(r), wk_h[ht]),
                           deps=[t_kmm.get(ht - 2)], dma=True)
            if ht >= 1:
                emit_rmm(ht - 1)
            t_wr[ht] = B.g("pool", lambda e, ht=ht: e.dma_start(wrb, wr_h[ht]),
                           deps=[t_rmm.get(ht - 1)], dma=True)
            emit_kmm(ht)
            emit_p1(3)
        emit_rmm(NHT - 1)
        t_outr_all = t_sblk[NHT - 1]
        t_kmm_all = t_kmm[NHT - 1]
        t_rmm_all = t_rmm[NHT - 1]

        if DEBUG:
            for ht in range(NHT):
                B.g("sp", lambda e, ht=ht: e.dma_start(
                    dbg["d_sr"][128 * ht:128 * (ht + 1), :], sr_t(ht)),
                    deps=[t_outr_all], dma=True)

        ypb = {0: PS[0], 1: PS[1]}
        t_wv, t_ymm, t_yevac = {}, {}, {}
        t_st23_0 = {}
        for ct in range(NCT):
            r = ct % 2
            t_wv[ct] = B.g("pool", lambda e, ct=ct, r=r: e.dma_start(wvb(r), wv_h[ct]),
                           deps=[t_ymm.get((ct - 2, 1)), t_kmm_all, t_rmm_all], dma=True)
            for tc in range(2):
                def f_y(e, r=r, tc=tc):
                    for ht in range(NHT):
                        last = e.matmul(ypb[tc][:, 0:512], wvb(r)[:, 128 * ht:128 * (ht + 1)],
                                        sr_t(ht)[:, 512 * tc:512 * (tc + 1)],
                                        start=(ht == 0), stop=(ht == NHT - 1))
                    return last
                t_ymm[(ct, tc)] = B.g("pe", f_y, deps=[t_wv[ct], t_outr_all,
                                                       t_yevac.get((ct - 1, tc))])
                t_yevac[(ct, tc)] = B.g("act", lambda e, ct=ct, tc=tc: e.copy(
                    y_t(ct)[:, 512 * tc:512 * (tc + 1)], ypb[tc][:, 0:512]),
                    deps=[t_ymm[(ct, tc)]])
                if tc == 0:
                    def f_yb0(e, ct=ct, r=r):
                        e.copy(ybf_r[r][:], y_t(ct)[:, 0:512])
                        return e.activation(ysq_r[r][:], y_t(ct)[:, 0:512], AF.Square)
                    t_yb0 = B.g("act", f_yb0, deps=[t_yevac[(ct, 0)],
                                                    t_st23_0.get(ct - 2)])
            def f_st0(e, ct=ct, r=r):
                e.matmul(PS[2][0:1, 0:512], G4[:, 4 * ct:4 * ct + 1], ybf_r[r][:],
                         start=(ct == 0), stop=(ct == NCT - 1))
                e.matmul(PS[3][0:1, 0:512], G4[:, 4 * ct + 1:4 * ct + 2], ybf_r[r][:],
                         start=(ct == 0), stop=(ct == NCT - 1))
                e.matmul(PS[4][0:1, 0:512], G4[:, 4 * ct + 2:4 * ct + 3], ybf_r[r][:],
                         start=(ct == 0), stop=(ct == NCT - 1))
                e.matmul(PS[5][0:1, 0:512], G4[:, 4 * ct + 3:4 * ct + 4], ybf_r[r][:],
                         start=(ct == 0), stop=(ct == NCT - 1))
                e.matmul(PS[6][0:1, 0:512], G2[:, 2 * ct:2 * ct + 1], ysq_r[r][:],
                         start=(ct == 0), stop=(ct == NCT - 1))
                return e.matmul(PS[7][0:1, 0:512], G2[:, 2 * ct + 1:2 * ct + 2], ysq_r[r][:],
                                start=(ct == 0), stop=(ct == NCT - 1))
            t_st23_0[ct] = B.g("pe", f_st0, deps=[t_yb0])
            emit_p1(4)
        t_ymm_all = t_ymm[(NCT - 1, 1)]
        while p1["i"] < n_p1_w13:
            emit_p1(4)

        if DEBUG:
            for ct in range(NCT):
                B.g("sp", lambda e, ct=ct: e.dma_start(
                    dbg["d_y"][128 * ct:128 * (ct + 1), :], y_t(ct)),
                    deps=[t_yevac[(ct, 0)], t_yevac[(ct, 1)]], dma=True)

        t_x2w = {}
        prev_tc_done = None
        t_ap4 = {}
        for tc in range(2):
            if tc == 0:
                t_st23 = t_st23_0
            else:
                t_st23 = {}
                for ct in range(NCT):
                    r = ct % 2
                    def f_yb(e, ct=ct, tc=tc, r=r):
                        e.copy(ybf_r[r][:], y_t(ct)[:, 512 * tc:512 * (tc + 1)])
                        return e.activation(ysq_r[r][:], y_t(ct)[:, 512 * tc:512 * (tc + 1)],
                                            AF.Square)
                    t_yb = B.g("act", f_yb, deps=[t_yevac[(ct, tc)], t_st23.get(ct - 2),
                                                  prev_tc_done])
                    def f_st(e, ct=ct, r=r):
                        e.matmul(PS[2][0:1, 0:512], G4[:, 4 * ct:4 * ct + 1], ybf_r[r][:],
                                 start=(ct == 0), stop=(ct == NCT - 1))
                        e.matmul(PS[3][0:1, 0:512], G4[:, 4 * ct + 1:4 * ct + 2], ybf_r[r][:],
                                 start=(ct == 0), stop=(ct == NCT - 1))
                        e.matmul(PS[4][0:1, 0:512], G4[:, 4 * ct + 2:4 * ct + 3], ybf_r[r][:],
                                 start=(ct == 0), stop=(ct == NCT - 1))
                        e.matmul(PS[5][0:1, 0:512], G4[:, 4 * ct + 3:4 * ct + 4], ybf_r[r][:],
                                 start=(ct == 0), stop=(ct == NCT - 1))
                        e.matmul(PS[6][0:1, 0:512], G2[:, 2 * ct:2 * ct + 1], ysq_r[r][:],
                                 start=(ct == 0), stop=(ct == NCT - 1))
                        return e.matmul(PS[7][0:1, 0:512], G2[:, 2 * ct + 1:2 * ct + 2],
                                        ysq_r[r][:],
                                        start=(ct == 0), stop=(ct == NCT - 1))
                    t_st23[ct] = B.g("pe", f_st, deps=[t_yb])
            sub_toks = []
            prev_bc = None
            for sc in range(0, 512, 256):
                nn = 256
                def f_rv1(e, sc=sc, nn=nn):
                    e.tensor_scalar(row("mu", nn), PS[2][0:1, sc:sc + nn], 1.0 / D, None,
                                    ALU.mult)
                    e.tensor_scalar(row("v1", nn), PS[6][0:1, sc:sc + nn], 1.0 / D, None,
                                    ALU.mult)
                    e.tensor_tensor(row("v2", nn), row("mu", nn), row("mu", nn), ALU.mult)
                    return e.tensor_tensor(row("v1", nn), row("v1", nn), row("v2", nn),
                                           ALU.subtract)
                trv1 = B.g("dve", f_rv1, deps=[t_st23[NCT - 1], t_sred, prev_bc])
                te1 = B.g("act", lambda e, nn=nn: (
                    e.activation(row("v2", nn), row("v1", nn), AF.Ln, bias=R_S["eps"]),
                    e.activation(row("ry", nn), row("v2", nn), AF.Exp, scale=-0.5))[-1],
                    deps=[trv1])
                mu = row("mu", nn); ry = row("ry", nn)
                t1 = row("v2", nn); t2 = row("brow", nn); t3 = row("rstd", nn)
                ps3 = PS[3][0:1, sc:sc + nn]; ps4 = PS[4][0:1, sc:sc + nn]
                ps5 = PS[5][0:1, sc:sc + nn]; ps7 = PS[7][0:1, sc:sc + nn]
                def f_rv2(e, mu=mu, ry=ry, t1=t1, t2=t2, t3=t3, ps3=ps3, ps4=ps4,
                          ps5=ps5, ps7=ps7):
                    e.scalar_tensor_tensor(t1, mu, R_S["S1"], ps3, ALU.mult, ALU.subtract)
                    e.scalar_tensor_tensor(t1, t1, -1.0, ry, ALU.mult, ALU.mult)
                    e.tensor_scalar(t2, t1, R_S["S2"], 1.0 / D, ALU.add, ALU.mult)
                    e.tensor_scalar(t3, ps4, 2.0, None, ALU.mult)
                    e.scalar_tensor_tensor(t3, mu, R_S["S3"], t3, ALU.mult, ALU.subtract)
                    e.tensor_tensor(t3, t3, mu, ALU.mult)
                    e.tensor_tensor(t3, ps7, t3, ALU.add)
                    e.tensor_tensor(t1, ry, ry, ALU.mult)
                    e.tensor_tensor(t3, t3, t1, ALU.mult)
                    e.scalar_tensor_tensor(t1, mu, R_S["S4"], ps5, ALU.mult, ALU.subtract)
                    e.scalar_tensor_tensor(t1, t1, -1.0, ry, ALU.mult, ALU.mult)
                    e.scalar_tensor_tensor(t3, t1, 2.0, t3, ALU.mult, ALU.add)
                    e.tensor_scalar(t3, t3, R_S["S5"], 1.0 / D, ALU.add, ALU.mult)
                    e.tensor_tensor(t1, t2, t2, ALU.mult)
                    return e.tensor_tensor(t3, t3, t1, ALU.subtract)
                trv2 = B.g("dve", f_rv2, deps=[te1])
                te2 = B.g("act", lambda e, nn=nn: (
                    e.activation(row("rstd", nn), row("rstd", nn), AF.Ln, bias=R_S["eps"]),
                    e.activation(row("rstd", nn), row("rstd", nn), AF.Exp, scale=-0.5))[-1],
                    deps=[trv2])
                def f_rv3(e, nn=nn):
                    e.tensor_tensor(row("v1", nn), row("ry", nn), row("rstd", nn), ALU.mult)
                    e.scalar_tensor_tensor(row("v2", nn), row("brow", nn), -1.0,
                                           row("rstd", nn), ALU.mult, ALU.mult)
                    return e.scalar_tensor_tensor(row("mu", nn), row("mu", nn), -1.0,
                                                  row("v1", nn), ALU.mult, ALU.mult)
                trv3 = B.g("dve", f_rv3, deps=[te2])
                def f_bce_cast(e, sc=sc, nn=nn):
                    e.copy(RWB3[0:1, sc:sc + nn], row("mu", nn))
                    e.copy(RWB[:, sc:sc + nn], row("rstd", nn))
                    return e.copy(RWB[:, 512 + sc:512 + sc + nn], row("v2", nn))
                tcast = B.g("act", f_bce_cast, deps=[trv3])
                prev_bc = B.g("pe", lambda e, sc=sc, nn=nn: e.matmul(
                    PS[2][:, sc:sc + nn], ONE_R[:], row("v1", nn), start=True, stop=True),
                    deps=[trv3, tcast])
                sub_toks.append((prev_bc, tcast))
            sub_pe = [t for (t, _) in sub_toks]
            sub_cast = [t for (_, t) in sub_toks]
            t_rep32 = B.g("sp", lambda e: e.dma_start(RWB3[1:2, :], RWB[:, 0:512]),
                          deps=[sub_cast[-1]], dma=True)
            t_rep64 = B.g("sp", lambda e: e.dma_start(RWB3[2:3, :], RWB[:, 512:1024]),
                          deps=[sub_cast[-1]], dma=True)
            t_bce23 = B.g("dve", lambda e: e.tensor_copy(bcA[:], PS[2][:, 0:512]),
                          deps=sub_pe)

            t_ap23 = {}
            t_wmm = {}
            t_st4 = {}
            wb_bank = lambda ct: PS[3 + (ct % 3)]
            for ct in range(NCT):
                r = ct % 2
                def f_w(e, ct=ct):
                    return e.matmul(wb_bank(ct)[:, 0:512], GROWB[0:3, 128 * ct:128 * (ct + 1)],
                                    RWB3[0:3, 0:512], start=True, stop=True)
                t_wmm[ct] = B.g("pe", f_w, deps=[t_rep32, t_rep64, sub_cast[-1]] +
                                t_growb_all + [t_ap23.get(ct - 3)])
                t_xres = B.g("sp", lambda e, ct=ct, tc=tc, r=r: e.dma_start(
                    xres3[r], xsh[128 * ct:128 * (ct + 1), 65 + 512 * tc:65 + 512 * (tc + 1)]),
                    deps=[t_ap23.get(ct - 2), t_bce23], dma=True)
                def f_ap(e, ct=ct, tc=tc, r=r):
                    yv = y_t(ct)[:, 512 * tc:512 * (tc + 1)]
                    e.tensor_tensor(aps3[:], yv, bcA[:], ALU.mult)
                    e.tensor_tensor(aps1[:], wb_bank(ct)[:, 0:512], xres3[r], ALU.add)
                    return e.scalar_tensor_tensor(yv, aps3[:], gam_c(ct), aps1[:],
                                                  ALU.mult, ALU.add)
                t_ap23[ct] = B.g("dve", f_ap, deps=[t_bce23, t_xres, t_wmm[ct],
                                                    t_ap23.get(ct - 1)])
                t_x2w[(ct, tc)] = B.g("sp", lambda e, ct=ct, tc=tc: e.dma_start(
                    yout[128 * ct:128 * (ct + 1), 512 * tc:512 * (tc + 1)],
                    y_t(ct)[:, 512 * tc:512 * (tc + 1)]), deps=[t_ap23[ct]], dma=True)
                def f_x2b(e, ct=ct, tc=tc, r=r):
                    e.copy(x2bf_r[r][:], y_t(ct)[:, 512 * tc:512 * (tc + 1)])
                    return e.activation(x2sq_r[r][:], y_t(ct)[:, 512 * tc:512 * (tc + 1)],
                                        AF.Square)
                t_xb = B.g("act", f_x2b, deps=[t_ap23[ct], t_st4.get(ct - 2)])
                def f_s4(e, ct=ct, r=r):
                    e.matmul(PS[6][0:1, 0:512], G4[:, 0:1], x2bf_r[r][:],
                             start=(ct == 0), stop=(ct == NCT - 1))
                    return e.matmul(PS[7][0:1, 0:512], G4[:, 0:1], x2sq_r[r][:],
                                    start=(ct == 0), stop=(ct == NCT - 1))
                t_st4[ct] = B.g("pe", f_s4, deps=[t_xb])
            tbc4 = ln_rows(PS[6], PS[7], 512, t_st4[NCT - 1], R_S["eps"], (PS[2], PS[3]),
                           t_bce23)
            def f_bce4(e):
                e.tensor_copy(bcA[:], PS[2][:, 0:512])
                return e.tensor_copy(bcB[:], PS[3][:, 0:512])
            t_bce4 = B.g("dve", f_bce4, deps=[tbc4])
            for ct in range(NCT):
                def f_a4(e, ct=ct, tc=tc):
                    e.tensor_tensor(aps1[:], y_t(ct)[:, 512 * tc:512 * (tc + 1)], bcA[:],
                                    ALU.mult)
                    return e.tensor_tensor(x2n_t(ct)[:, 512 * tc:512 * (tc + 1)], aps1[:],
                                           bcB[:], ALU.add)
                t_ap4[(tc, ct)] = B.g("dve", f_a4, deps=[t_bce4, p1["w13_done"],
                                                         t_x2w[(ct, tc)],
                                                         t_ap4.get((tc, ct - 1))])
            prev_tc_done = t_ap4[(tc, NCT - 1)]
        t_ap4_all = prev_tc_done

        if DEBUG:
            for ct in range(NCT):
                B.g("sp", lambda e, ct=ct: e.dma_start(
                    dbg["d_x2n"][128 * ct:128 * (ct + 1), :], x2n_t(ct)),
                    deps=[t_ap4_all], dma=True)
                B.g("sp", lambda e, ct=ct: e.dma_start(
                    dbg["d_x2"][128 * ct:128 * (ct + 1), :], y_t(ct)),
                    deps=[t_ap4_all], dma=True)

        def s_finalize(acc_lo, acc_hi, dep, thr_ap, nthr_ap, s_slot, t1, t2, acc_col,
                       extra=None):
            tred = B.g("dve", lambda e: e.tensor_reduce(
                ACC[:, acc_col:acc_col + 1], ACC[:, acc_lo:acc_hi], AX.X, ALU.add),
                deps=[dep, extra])
            tmm = B.g("pe", lambda e: e.matmul(PS[6][0:1, 0:1], ONE_C[:],
                                               ACC[:, acc_col:acc_col + 1],
                                               start=True, stop=True), deps=[tred])
            ts_ = B.g("dve", lambda e: e.tensor_scalar(
                s_slot, PS[6][0:1, 0:1], 1.0 / (F_REAL * D), None, ALU.mult), deps=[tmm],
                settle=1)
            tt1 = B.g("dve", lambda e: e.tensor_scalar(t1, s_slot, 0.5, None, ALU.mult),
                      deps=[ts_], settle=1)
            tt2 = B.g("dve", lambda e: e.tensor_scalar(t2, t1, -1.0, None, ALU.mult),
                      deps=[tt1])
            def f_bc(e):
                e.matmul(PS[6][:, 0:1], ONE_R[:], t1, start=True, stop=True)
                return e.matmul(PS[7][:, 0:1], ONE_R[:], t2, start=True, stop=True)
            tbc = B.g("pe", f_bc, deps=[tt2])
            def f_e(e):
                e.tensor_copy(thr_ap, PS[6][:, 0:1])
                return e.tensor_copy(nthr_ap, PS[7][:, 0:1])
            return B.g("dve", f_e, deps=[tbc])

        t_thr1 = s_finalize(0, n_p1_w1, p1["w1_done"], thr1c, nthr1c, R_S["s1"],
                            R_S["t1a"], R_S["t2a"], 192)
        t_thr3 = s_finalize(n_p1_w1, n_p1_w13, p1["w13_done"], thr3c, nthr3c, R_S["s3"],
                            R_S["t1b"], R_S["t2b"], 193)
        t_s1bc = B.g("pe", lambda e: e.matmul(PS[6][:, 0:1], ONE_R[:], R_S["s1"],
                                              start=True, stop=True), deps=[t_thr1, t_thr3])
        t_s1col = B.g("dve", lambda e: e.tensor_copy(s1col, PS[6][:, 0:1]), deps=[t_s1bc])

        h1b = {0: (PS[0], PS[1]), 1: (PS[4], PS[5])}
        h3b = {0: (PS[2], PS[3]), 1: (PS[6], PS[7])}
        t_q1, t_q3, t_h1, t_h3, t_silu, t_g = {}, {}, {}, {}, {}, {}
        qst = {"last_tb1": None, "last_tb3": None,
               "stg1": {}, "stg3": {}}

        def emit_q(ft, which):
            r = ft % 2
            if which == 1:
                wh, stg, qt, qbv, thr, nthr, tthr = (w1_h, wstg1, qtmp1, q1b_t, thr1c, nthr1c,
                                                     t_thr1)
                stg_used, last_key, tq, war_qb = qst["stg1"], "last_tb1", t_q1, t_h1.get((ft - 2, 1))
            else:
                wh, stg, qt, qbv, thr, nthr, tthr = (w3_h, wstg3, qtmp3, q3b_t, thr3c, nthr3c,
                                                     t_thr3)
                stg_used, last_key, tq, war_qb = qst["stg3"], "last_tb3", t_q3, t_h3.get((ft - 2, 1))
            tb = None
            for h_ in range(2):
                td = B.g("sp", lambda e, ft=ft, h_=h_, wh=wh, stg=stg: e.dma_start(
                    stg[h_], wh[ft, :, 1024 * h_:1024 * (h_ + 1)]),
                    deps=[stg_used.get(h_), t_ymm_all], dma=True)
                ta = B.g("dve", lambda e, h_=h_, stg=stg, qt=qt, nthr=nthr:
                         e.tensor_single_scalar(qt, stg[h_], nthr, ALU.is_lt),
                         deps=[td, tthr, qst[last_key], war_qb])
                tb = B.g("dve", lambda e, r=r, h_=h_, stg=stg, qt=qt, qbv=qbv, thr=thr:
                         e.scalar_tensor_tensor(qbv(r)[:, 1024 * h_:1024 * (h_ + 1)], stg[h_],
                                                thr, qt, ALU.is_gt, ALU.subtract),
                         deps=[ta])
                stg_used[h_] = tb
                qst[last_key] = tb
            tq[ft] = tb

        emit_q(0, 1)
        emit_q(0, 3)
        t_ap4_tc = {0: t_ap4[(0, NCT - 1)], 1: t_ap4[(1, NCT - 1)]}
        for ft in range(NFT):
            r = ft % 2
            if ft + 1 < NFT:
                emit_q(ft + 1, 1)
                emit_q(ft + 1, 3)
            bb1 = h1b[r]
            bb3 = h3b[r]
            for tc in range(2):
                def f_h1(e, r=r, bb1=bb1, tc=tc):
                    for ct in range(NCT):
                        last = e.matmul(bb1[tc][:, 0:512], q1b_t(r)[:, 128 * ct:128 * (ct + 1)],
                                        x2n_t(ct)[:, 512 * tc:512 * (tc + 1)],
                                        start=(ct == 0), stop=(ct == NCT - 1))
                    return last
                t_h1[(ft, tc)] = B.g("pe", f_h1, deps=[t_q1[ft], t_ap4_tc[tc],
                                                       t_silu.get((ft - 2, tc))])
                def f_h3(e, r=r, bb3=bb3, tc=tc):
                    for ct in range(NCT):
                        last = e.matmul(bb3[tc][:, 0:512], q3b_t(r)[:, 128 * ct:128 * (ct + 1)],
                                        x2n_t(ct)[:, 512 * tc:512 * (tc + 1)],
                                        start=(ct == 0), stop=(ct == NCT - 1))
                    return last
                t_h3[(ft, tc)] = B.g("pe", f_h3, deps=[t_q3[ft], t_ap4_tc[tc],
                                                       t_g.get((ft - 2, tc)),
                                                       t_s1col if ft == 1 else None])
                t_silu[(ft, tc)] = B.g("act", lambda e, r=r, bb1=bb1, tc=tc: e.activation(
                    silu_t(r)[:, 512 * tc:512 * (tc + 1)], bb1[tc][:, 0:512], AF.Silu,
                    scale=s1col),
                    deps=[t_h1[(ft, tc)], t_s1col, t_g.get((ft - 2, tc))])
                t_g[(ft, tc)] = B.g("dve", lambda e, ft=ft, r=r, bb3=bb3, tc=tc: e.tensor_tensor(
                    g_t(ft)[:, 512 * tc:512 * (tc + 1)], silu_t(r)[:, 512 * tc:512 * (tc + 1)],
                    bb3[tc][:, 0:512], ALU.mult),
                    deps=[t_silu[(ft, tc)], t_h3[(ft, tc)]])
            emit_p1(2, allow_w2=True, extra_dep=t_ymm_all)
        t_g_all = t_g[(NFT - 1, 1)]
        t_g_tc = {0: t_g[(NFT - 1, 0)], 1: t_g[(NFT - 1, 1)]}

        if DEBUG:
            for ft in range(NFT):
                B.g("sp", lambda e, ft=ft: e.dma_start(
                    dbg["d_g"][128 * ft:128 * (ft + 1), :], g_t(ft)), deps=[t_g_all], dma=True)

        while p1["i"] < n_p1_all:
            emit_p1(4, allow_w2=True, extra_dep=t_ymm_all)
        t_thr2 = s_finalize(n_p1_w13, n_p1_all, p1["all_done"], thr2c, nthr2c, R_S["s2"],
                            R_S["t1c"], R_S["t2c"], 194, extra=t_g_all)

        if DEBUG:
            B.g("sp", lambda e: e.dma_start(dbg["d_acc"], ACC[:]),
                deps=[t_thr2], dma=True)
            B.g("sp", lambda e: e.dma_start(dbg["d_scl"], SCL[:]),
                deps=[t_thr2, t_s1col], dma=True)
            B.g("sp", lambda e: e.dma_start(dbg["d_rw"], RW[:]),
                deps=[t_thr2], dma=True)

        te5a = B.g("dve", lambda e: e.tensor_tensor(R_S["t1e"], R_S["s2"], R_S["s3"],
                                                    ALU.mult), deps=[t_thr2, t_thr3],
                   settle=1)
        te5b = B.g("dve", lambda e: e.tensor_tensor(R_S["t1e"], R_S["t1e"], R_S["t1e"],
                                                    ALU.mult), deps=[te5a], settle=1)
        te5c = B.g("dve", lambda e: e.reciprocal(R_S["t2e"], R_S["t1e"]), deps=[te5b],
                   settle=1)
        t_eps5 = B.g("dve", lambda e: e.tensor_scalar(R_S["eps5"], R_S["t2e"], EPS, None,
                                                      ALU.mult), deps=[te5c])

        y2b = {0: PS[0], 1: PS[1]}
        t_q2, t_y2, t_y2bf, t_st5 = {}, {}, {}, {}
        q2st = {"last_tb": None, "stg": {}}

        def emit_q2(ct):
            r = ct % 2
            tb = None
            for h_ in range(6):
                lo = 1024 * h_
                hi = min(lo + 1024, 5504)
                n = hi - lo
                rr = h_ % 2
                td = B.g("sp", lambda e, ct=ct, rr=rr, lo=lo, hi=hi, n=n: e.dma_start(
                    wstg2[rr][:, 0:n], w2_h[ct, :, lo:hi]),
                    deps=[q2st["stg"].get(rr), t_g_all], dma=True)
                ta = B.g("dve", lambda e, rr=rr, n=n: e.tensor_single_scalar(
                    qtmp2[:, 0:n], wstg2[rr][:, 0:n], nthr2c, ALU.is_lt),
                    deps=[td, t_thr2, q2st["last_tb"], t_y2.get((ct - 2, 1))])
                tb = B.g("dve", lambda e, r=r, rr=rr, lo=lo, hi=hi, n=n: e.scalar_tensor_tensor(
                    q2b_t(r)[:, lo:hi], wstg2[rr][:, 0:n], thr2c, qtmp2[:, 0:n],
                    ALU.is_gt, ALU.subtract), deps=[ta])
                q2st["stg"][rr] = tb
                q2st["last_tb"] = tb
            t_q2[ct] = tb

        emit_q2(0)
        for ct in range(NCT):
            r = ct % 2
            if ct + 1 < NCT:
                emit_q2(ct + 1)
            for tc in range(2):
                def f_y2(e, r=r, tc=tc):
                    for ft in range(NFT):
                        last = e.matmul(y2b[tc][:, 0:512], q2b_t(r)[:, 128 * ft:128 * (ft + 1)],
                                        g_t(ft)[:, 512 * tc:512 * (tc + 1)],
                                        start=(ft == 0), stop=(ft == NFT - 1))
                    return last
                t_y2[(ct, tc)] = B.g("pe", f_y2, deps=[t_q2[ct], t_g_tc[tc],
                                                       t_y2bf.get((ct - 1, tc))])
                def f_y2b(e, ct=ct, tc=tc):
                    e.copy(y2bf_t(ct)[:, 512 * tc:512 * (tc + 1)], y2b[tc][:, 0:512])
                    return e.activation(y2sq_r[tc][:], y2b[tc][:, 0:512], AF.Square)
                t_y2bf[(ct, tc)] = B.g("act", f_y2b, deps=[t_y2[(ct, tc)],
                                                           t_st5.get(ct - 1)])
            def f_s5(e, ct=ct):
                e.matmul(PS[2][0:1, 0:512], G4[:, 0:1], y2bf_t(ct)[:, 0:512],
                         start=(ct == 0), stop=(ct == NCT - 1))
                e.matmul(PS[3][0:1, 0:512], G4[:, 0:1], y2bf_t(ct)[:, 512:1024],
                         start=(ct == 0), stop=(ct == NCT - 1))
                e.matmul(PS[4][0:1, 0:512], G4[:, 0:1], y2sq_r[0][:],
                         start=(ct == 0), stop=(ct == NCT - 1))
                return e.matmul(PS[5][0:1, 0:512], G4[:, 0:1], y2sq_r[1][:],
                                start=(ct == 0), stop=(ct == NCT - 1))
            t_st5[ct] = B.g("pe", f_s5, deps=[t_y2bf[(ct, 0)], t_y2bf[(ct, 1)]])

        t_ap5 = {}
        xre_user = {}
        out_user = {}
        prev5 = None
        for tc in range(2):
            pSy = PS[2] if tc == 0 else PS[3]
            pS2 = PS[4] if tc == 0 else PS[5]
            tbc5 = ln_rows(pSy, pS2, 512, t_st5[NCT - 1], R_S["eps5"], (PS[6], PS[7]), prev5)
            def f_bce5(e):
                e.tensor_copy(bcA[:], PS[6][:, 0:512])
                return e.tensor_copy(bcB[:], PS[7][:, 0:512])
            t_bce5 = B.g("dve", f_bce5, deps=[tbc5, t_eps5])
            for ct in range(NCT):
                r3 = ct % 3
                r2 = ct % 2
                t_xre = B.g("sp", lambda e, ct=ct, tc=tc, r3=r3: e.dma_start(
                    xre6[r3], yout[128 * ct:128 * (ct + 1), 512 * tc:512 * (tc + 1)]),
                    deps=[t_x2w[(ct, tc)], xre_user.get(r3), t_y2bf[(NCT - 1, 1)]], dma=True)
                def f_a5(e, ct=ct, tc=tc, r3=r3, r2=r2):
                    e.tensor_tensor(aps6[:], y2bf_t(ct)[:, 512 * tc:512 * (tc + 1)], bcA[:],
                                    ALU.mult)
                    e.tensor_tensor(aps6[:], aps6[:], bcB[:], ALU.add)
                    return e.tensor_tensor(out6[r2], aps6[:], xre6[r3], ALU.add)
                t_ap5[(tc, ct)] = B.g("dve", f_a5, deps=[t_bce5, t_xre, out_user.get(r2),
                                                         t_ap5.get((tc, ct - 1))])
                xre_user[r3] = t_ap5[(tc, ct)]
                out_user[r2] = B.g("sp", lambda e, ct=ct, tc=tc, r2=r2: e.dma_start(
                    yout[128 * ct:128 * (ct + 1), 512 * tc:512 * (tc + 1)], out6[r2]),
                    deps=[t_ap5[(tc, ct)]], dma=True)
            prev5 = t_ap5[(tc, NCT - 1)]

        awaited = set()
        for e in ENGINES:
            for (fn, deps, sem, amt, settle, v) in B.groups[e]:
                for (s, dv) in deps:
                    if s != e:
                        awaited.add((s, dv))
        sp_finals = [(f"d_sp{ln}", B.counts.get(f"d_sp{ln}", 0)) for ln in range(N_LANES["sp"])]

        with nc.Block("main") as blk:
            def body_for(name):
                def body(eng):
                    waited = {}
                    for (fn, deps, sem, amt, settle, v) in B.groups[name]:
                        for (s, dv) in deps:
                            if waited.get(s, 0) < dv:
                                eng.wait_ge(sems[s], dv)
                                waited[s] = dv
                        last = fn(eng)
                        if amt == 16:
                            last.then_inc(sems[sem], amt)
                        elif (sem, v) in awaited or settle:
                            eng.drain(fusable=True).then_inc(sems[sem], amt)
                        else:
                            last.then_inc(sems[sem], amt)
                    if name == "sp":
                        for s, fv in sp_finals:
                            if fv:
                                eng.wait_ge(sems[s], fv)
                return body
            blk.sync(body_for("sp"))
            blk.tensor(body_for("pe"))
            blk.vector(body_for("dve"))
            blk.scalar(body_for("act"))
            blk.gpsimd(body_for("pool"))

    return nc


_PROG_CACHE = {}


def _prep_host(inputs):
    x = np.asarray(inputs["L0_attn_x"], np.float32)
    Wk = np.asarray(inputs["L0_attn_Wk"], np.float32)
    Wr = np.asarray(inputs["L0_attn_Wr"], np.float32)
    Wv = np.asarray(inputs["L0_attn_Wv"], np.float32)
    tmk = np.asarray(inputs["L0_attn_time_mix_k"], np.float32)
    tmr = np.asarray(inputs["L0_attn_time_mix_r"], np.float32)
    dec = np.asarray(inputs["L0_attn_time_decay"], np.float32)
    gam = np.asarray(inputs["L0_attn_ln_gamma"], np.float32)
    bet = np.asarray(inputs["L0_attn_ln_beta"], np.float32)
    w1 = np.asarray(inputs["L0_ffn_w1"], np.float32)
    w3 = np.asarray(inputs["L0_ffn_w3"], np.float32)
    w2 = np.asarray(inputs["L0_ffn_w2"], np.float32)

    wk_img = np.ascontiguousarray(
        Wk.reshape(NHT, 128, NCT, 128).transpose(0, 3, 2, 1)).reshape(NHT, 128, 2048)
    wr_img = np.ascontiguousarray(
        Wr.reshape(NHT, 128, NCT, 128).transpose(0, 3, 2, 1)).reshape(NHT, 128, 2048)
    wv_img = np.ascontiguousarray(
        Wv.reshape(NCT, 128, NHT, 128).transpose(0, 3, 2, 1)).reshape(NCT, 128, 2816)
    w1p = np.zeros((FP, D), np.float32); w1p[:F_REAL] = w1
    w3p = np.zeros((FP, D), np.float32); w3p[:F_REAL] = w3
    w2p = np.zeros((D, FP), np.float32); w2p[:, :F_REAL] = w2
    w1_img = np.ascontiguousarray(
        w1p.reshape(NFT, 128, NCT, 128).transpose(0, 3, 2, 1)).reshape(NFT, 128, 2048)
    w3_img = np.ascontiguousarray(
        w3p.reshape(NFT, 128, NCT, 128).transpose(0, 3, 2, 1)).reshape(NFT, 128, 2048)
    w2_img = np.ascontiguousarray(
        w2p.reshape(NCT, 128, NFT, 128).transpose(0, 3, 2, 1)).reshape(NCT, 128, 5504)

    gbr = np.zeros((3, 2048), np.float32)
    gbr[0] = gam; gbr[1] = bet; gbr[2] = 1.0

    vec_base = np.zeros((128, 152), np.float32)
    vec_base[:, 0:16] = tmk.reshape(16, 128).T
    vec_base[:, 16:32] = tmr.reshape(16, 128).T
    vec_base[:, 32:48] = gam.reshape(16, 128).T
    vec_base[:, 48:64] = bet.reshape(16, 128).T
    vec_base[:, 64:86] = dec.reshape(22, 128).T

    in_maps = []
    for core in range(8):
        b, j = divmod(core, 4)
        if j == 0:
            rows = np.concatenate([np.repeat(x[b, 1:2], 65, 0), x[b, 0:1024]], 0)
            m = 0.0
        else:
            rows = x[b, j * 1024 - 65: j * 1024 + 1024]
            m = 1.0
        xshard = np.ascontiguousarray(rows.T)
        v = vec_base.copy()
        v[:, 86:150] = m
        in_maps.append({
            "xsh": xshard, "vecs": v,
            "wk_h": wk_img, "wr_h": wr_img, "wv_h": wv_img,
            "w1_h": w1_img, "w3_h": w3_img, "w2_h": w2_img, "gbrow": gbr,
        })
    return in_maps


def kernel(**inputs):
    in_maps = _prep_host(inputs)
    if "nc" not in _PROG_CACHE:
        _PROG_CACHE["nc"] = build_program()
    nc = _PROG_CACHE["nc"]
    res = run_bass_kernel_spmd(nc, in_maps, core_ids=list(range(8)))
    out = np.empty((2, 4096, 2048), np.float32)
    for core in range(8):
        b, j = divmod(core, 4)
        out[b, j * 1024:(j + 1) * 1024, :] = np.asarray(res.results[core]["yout"],
                                                        dtype=np.float32).T
    return out


# version 17 (final)
# speedup vs baseline: 1.0519x; 1.0519x over previous
import numpy as np
import concourse.bass as bass
import concourse.mybir as mybir
from concourse.bass_utils import run_bass_kernel_spmd
from contextlib import ExitStack

F32 = mybir.dt.float32
BF16 = mybir.dt.bfloat16
AF = mybir.ActivationFunctionType
ALU = mybir.AluOpType
AX = mybir.AxisListType

D = 2048
F_REAL = 5461
FP = 5504
NCT = 16
NHT = 22
NFT = 43
T = 1024
W = 64
TL = 1089
TS_ = 1088
EPS = 1e-5
CH_X = [(0, 512), (512, 1024), (1024, 1089)]
CH_S = [(0, 512), (512, 1024), (1024, 1088)]
DEBUG = False

ENGINES = ("sp", "pe", "dve", "act", "pool")


N_LANES = {"sp": 6, "pool": 4, "act": 2, "dve": 2, "pe": 2}


class Builder:

    def __init__(self):
        self.groups = {e: [] for e in ENGINES}
        self.counts = {}
        self.lane_rr = {}
        self.lane_last = {}

    def g(self, engine, fn, deps=(), dma=False, settle=0):
        deps = [t for t in deps if t is not None]
        if dma:
            lane = self.lane_rr.get(engine, 0)
            self.lane_rr[engine] = (lane + 1) % N_LANES[engine]
            sem = f"d_{engine}{lane}"
            amt = 16
            prev = self.lane_last.get(sem)
            if prev is not None:
                deps.append(prev)
        else:
            sem = engine
            amt = 1
        v = self.counts.get(sem, 0) + amt
        self.counts[sem] = v
        self.groups[engine].append((fn, tuple(deps), sem, amt, settle, v))
        tok = (sem, v)
        if dma:
            self.lane_last[sem] = tok
        return tok

    def seq(self, engine, fns, deps=()):
        tok = None
        for i, f in enumerate(fns):
            tok = self.g(engine, f, deps=(list(deps) + [tok]) if i == 0 else [tok])
        return tok


def build_program():
    nc = bass.Bass("TRN2", target_bir_lowering=False)

    xsh = nc.dram_tensor("xsh", [D, TL], F32, kind="ExternalInput").ap()
    vecs = nc.dram_tensor("vecs", [128, 152], F32, kind="ExternalInput").ap()
    wk_h = nc.dram_tensor("wk_h", [NHT, 128, 2048], F32, kind="ExternalInput").ap()
    wr_h = nc.dram_tensor("wr_h", [NHT, 128, 2048], F32, kind="ExternalInput").ap()
    wv_h = nc.dram_tensor("wv_h", [NCT, 128, 2816], F32, kind="ExternalInput").ap()
    w1_h = nc.dram_tensor("w1_h", [NFT, 128, 2048], F32, kind="ExternalInput").ap()
    w3_h = nc.dram_tensor("w3_h", [NFT, 128, 2048], F32, kind="ExternalInput").ap()
    w2_h = nc.dram_tensor("w2_h", [NCT, 128, 5504], F32, kind="ExternalInput").ap()
    gbrow = nc.dram_tensor("gbrow", [3, 2048], F32, kind="ExternalInput").ap()
    yout = nc.dram_tensor("yout", [D, T], F32, kind="ExternalOutput").ap()
    dbg = {}
    if DEBUG:
        dbg["d_sr"] = nc.dram_tensor("d_sr", [NHT * 128, T], BF16, kind="ExternalOutput").ap()
        dbg["d_y"] = nc.dram_tensor("d_y", [D, T], F32, kind="ExternalOutput").ap()
        dbg["d_x2n"] = nc.dram_tensor("d_x2n", [D, T], BF16, kind="ExternalOutput").ap()
        dbg["d_x2"] = nc.dram_tensor("d_x2", [D, T], F32, kind="ExternalOutput").ap()
        dbg["d_g"] = nc.dram_tensor("d_g", [NFT * 128, T], BF16, kind="ExternalOutput").ap()
        dbg["d_acc"] = nc.dram_tensor("d_acc", [128, 200], F32, kind="ExternalOutput").ap()
        dbg["d_scl"] = nc.dram_tensor("d_scl", [128, 8], F32, kind="ExternalOutput").ap()
        dbg["d_rw"] = nc.dram_tensor("d_rw", [1, 1600], F32, kind="ExternalOutput").ap()

    B = Builder()

    with ExitStack() as st:
        sb = lambda n, s, d=F32: st.enter_context(nc.sbuf_tensor(n, s, d))
        A1 = sb("A1", [128, 22016])
        A2 = sb("A2", [128, 8192])
        A4 = sb("A4", [128, 11264])
        A3 = sb("A3", [128, 3072])
        BC = sb("BC", [128, 2178])
        STG = sb("STG", [128, 1536])
        RW = sb("RW", [1, 1600])
        VEC = sb("VEC", [128, 152])
        WDEC = sb("WDEC", [128, 22])
        DTMP = sb("DTMP", [128, 22])
        G4 = sb("G4", [128, 64], BF16)
        G2 = sb("G2", [128, 32], BF16)
        GB = sb("GB", [128, 80], BF16)
        ONE_R = sb("ONE_R", [1, 128])
        ONE_C = sb("ONE_C", [128, 1])
        ACC = sb("ACC", [128, 200])
        SCL = sb("SCL", [128, 8])
        X1R = sb("X1R", [128, 2180], BF16)
        GROWB = sb("GROWB", [128, 2048], BF16)

        PS = [st.enter_context(nc.psum_tensor(f"PS{i}", [128, 512], F32)) for i in range(8)]

        sems = {}
        for e in ENGINES:
            sems[e] = st.enter_context(nc.semaphore("s_" + e))
            for ln in range(N_LANES[e]):
                sems[f"d_{e}{ln}"] = st.enter_context(nc.semaphore(f"sd_{e}{ln}"))

        def cv(arena, c0, n):
            return arena[:, c0:c0 + n]

        def cb(arena, c0, nbf):
            return arena[:, c0:c0 + (nbf + 1) // 2].bitcast(BF16)[:, :nbf]

        xt = lambda ct: cv(A1, 1089 * ct, 1089)
        xm_base = lambda ct: 1089 * (ct - 1) if ct > 0 else 17424
        xmk = lambda ct: cb(A1, xm_base(ct), 1088)
        xmr = lambda ct: cb(A1, xm_base(ct) + 544, 1088)
        d_rot = [cb(A1, 18512 + 544 * i, 1088) for i in range(2)]
        a_sb = [cv(A1, 19600 + 1088 * i, 1088) for i in range(2)]
        aps_ph1 = cv(A1, 19600, 512)
        y_t = lambda ct: cv(A1, 1024 * ct, 1024)
        g_t = lambda ft: cb(A1, 512 * ft, 1024)
        aps1 = cv(A1, 18944, 512)
        aps2 = cv(A1, 19456, 512)
        aps3 = cv(A1, 19968, 512)
        xres3 = [cv(A1, 20992 + 512 * i, 512) for i in range(2)]

        wabs = [cv(A2, 2048 * i, 2048) for i in range(2)]
        state_t = cv(A2, 4096, 1088)
        x2n_t = lambda ct: cb(A2, 512 * ct, 1024)
        y2bf_t = lambda ct: cb(A2, 512 * ct, 1024)

        sr_t = lambda ht: cb(A4, 512 * ht, 1024)
        wstg1 = [cv(A4, 1024 * i, 1024) for i in range(2)]
        wstg3 = [cv(A4, 2048 + 1024 * i, 1024) for i in range(2)]
        qtmp1 = cv(A4, 4096, 1024)
        qtmp3 = cv(A4, 5120, 1024)
        q1b_t = lambda i: cb(A4, 6144 + 1024 * i, 2048)
        q3b_t = lambda i: cb(A4, 8192 + 1024 * i, 2048)
        p1w2 = cv(A4, 10240, 1024)
        q2b_t = lambda i: cb(A4, 2752 * i, 5504)
        wstg2 = [cv(A4, 5504 + 1024 * i, 1024) for i in range(2)]
        qtmp2 = cv(A4, 7552, 1024)
        xre6 = [cv(A4, 8576 + 512 * i, 512) for i in range(3)]
        out6 = [cv(A4, 10112 + 512 * i, 512) for i in range(2)]
        aps6 = cv(A4, 7552, 512)

        wkb = lambda i: cb(A3, 1024 * i, 2048)
        wrb = cb(A3, 2048, 2048)
        wvb = lambda i: cb(A3, 1408 * i, 2816)

        aCh = [cv(BC, o, n) for (o, n) in [(0, 512), (512, 512), (1024, 65)]]
        bCh = [cv(BC, o, n) for (o, n) in [(1089, 512), (1601, 512), (2113, 65)]]
        RWB = BC[0:1, 1024:1792].bitcast(BF16)
        RWB3 = BC[0:3, 1792:2048].bitcast(BF16)
        bcA, bcB, bcC, bcE = cv(BC, 0, 512), cv(BC, 512, 512), cv(BC, 1024, 512), cv(BC, 1536, 512)

        ybf_r = [cb(STG, 256 * i, 512) for i in range(2)]
        ysq_r = [cb(STG, 512 + 256 * i, 512) for i in range(2)]
        cb1 = ybf_r
        sb1 = ysq_r
        x2bf_r = [cb(STG, 1024 + 256 * i, 512) for i in range(2)]
        x2sq_r = [cb(STG, 256 * i, 512) for i in range(2)]
        silu_t = lambda i: cb(STG, 512 * i, 1024)
        y2sq_r = [cb(STG, 512 + 256 * i, 512) for i in range(2)]

        R_S = {k: RW[0:1, i:i + 1] for i, k in enumerate(
            ["S1", "S2", "S3", "S4", "S5", "eps", "eps5", "s1", "s3", "s2",
             "t1a", "t2a", "t1b", "t2b", "t1c", "t2c", "t1e", "t2e"])}
        ROWN = ["mu", "v1", "v2", "rstd", "brow", "ry"]
        ROW = {k: RW[0:1, 32 + 256 * i:32 + 256 * (i + 1)] for i, k in enumerate(ROWN)}
        row = lambda k, n: RW[0:1, 32 + 256 * ROWN.index(k):32 + 256 * ROWN.index(k) + n]

        tmk_c = lambda ct: VEC[:, ct:ct + 1]
        tmr_c = lambda ct: VEC[:, 16 + ct:17 + ct]
        gam_c = lambda ct: VEC[:, 32 + ct:33 + ct]
        bet_c = lambda ct: VEC[:, 48 + ct:49 + ct]
        dec_all = VEC[:, 64:86]
        mask_t = VEC[:, 86:150]
        thr1c, nthr1c = SCL[:, 0:1], SCL[:, 1:2]
        thr3c, nthr3c = SCL[:, 2:3], SCL[:, 3:4]
        thr2c, nthr2c = SCL[:, 4:5], SCL[:, 5:6]
        s1col = SCL[:, 6:7]

        t_vec = B.g("sp", lambda e: e.dma_start(VEC[:], vecs), dma=True)
        t_dec1 = B.g("act", lambda e: e.activation(DTMP[:], dec_all, AF.Exp), deps=[t_vec])
        t_wdec = B.g("act", lambda e: e.activation(WDEC[:], DTMP[:], AF.Exp, scale=-1.0),
                     deps=[t_dec1])

        def f_cv1(e):
            e.memset(ACC[:], 0.0)
            e.memset(SCL[:], 0.0)
            e.memset(RW[:], 0.0)
            e.memset(ONE_R[:], 1.0)
            e.memset(ONE_C[:], 1.0)
            e.memset(G4[:], 1.0)
            return e.memset(G2[:], 1.0)
        def f_cv2(e):
            e.memset(R_S["eps"], EPS)
            e.tensor_copy(G4[:, 1:64:4], VEC[:, 32:48])
            e.tensor_tensor(G4[:, 3:64:4], VEC[:, 32:48], VEC[:, 48:64], ALU.mult)
            e.tensor_copy(GB[:, 0:16], VEC[:, 32:48])
            e.tensor_copy(GB[:, 16:32], VEC[:, 48:64])
            return e.tensor_tensor(GB[:, 48:64], VEC[:, 32:48], VEC[:, 48:64], ALU.mult)
        t_constv = B.seq("dve", [f_cv1, f_cv2], deps=[t_vec])

        def f_const_a(e):
            e.activation(G4[:, 2:64:4], VEC[:, 32:48], AF.Square)
            e.activation(G2[:, 1:32:2], VEC[:, 32:48], AF.Square)
            e.activation(GB[:, 32:48], VEC[:, 32:48], AF.Square)
            return e.activation(GB[:, 64:80], VEC[:, 48:64], AF.Square)
        t_consta = B.g("act", f_const_a, deps=[t_vec, t_constv])

        t_smm = B.g("pe", lambda e: e.matmul(PS[7][0:1, 0:80], G4[:, 0:1], GB[:],
                                             start=True, stop=True),
                    deps=[t_consta, t_constv])

        tg0 = B.g("pool", lambda e: e.dma_start(GROWB[0:1, :], gbrow[0:1, :]), dma=True)
        tg1 = B.g("pool", lambda e: e.dma_start(GROWB[1:2, :], gbrow[1:2, :]), dma=True)
        tg2 = B.g("pool", lambda e: e.dma_start(GROWB[2:3, :], gbrow[2:3, :]), dma=True)
        t_growb_all = [tg0, tg1, tg2]

        def f_sred(e):
            for i, k in enumerate(["S1", "S2", "S3", "S4", "S5"]):
                last = e.tensor_reduce(R_S[k], PS[7][0:1, 16 * i:16 * (i + 1)], AX.X, ALU.add)
            return last
        t_sred = B.g("dve", f_sred, deps=[t_smm])

        p1_tiles = []
        for m, wh in ((1, w1_h), (3, w3_h)):
            for ft in range(NFT):
                p1_tiles.append((wh[ft], 2048, "w13"))
        n_p1_w1 = NFT
        n_p1_w13 = NFT * 2
        for ct in range(NCT):
            for h_ in range(6):
                lo = 1024 * h_
                hi = min(lo + 1024, 5504)
                p1_tiles.append((w2_h[ct, :, lo:hi], hi - lo, "w2"))
        n_p1_all = len(p1_tiles)
        p1 = {"i": 0, "tok": [None, None], "w2tok": None,
              "w1_done": None, "w13_done": None, "all_done": None}

        def emit_p1(nslices, allow_w2=False, extra_dep=None):
            for _ in range(nslices):
                i = p1["i"]
                if i >= n_p1_all:
                    return
                src, ncols, kind = p1_tiles[i]
                if kind == "w2" and not allow_w2:
                    return
                if kind == "w13":
                    r = i % 2
                    buf = wabs[r]
                    war = p1["tok"][r]
                else:
                    r = 0
                    buf = p1w2
                    war = p1["w2tok"]
                td = B.g("sp", lambda e, src=src, buf=buf, ncols=ncols: e.dma_start(
                    buf[:, :ncols], src), deps=[war, extra_dep], dma=True)
                ta = B.g("act", lambda e, buf=buf, i=i, ncols=ncols: e.activation(
                    buf[:, :ncols], buf[:, :ncols], AF.Abs, accum_out=ACC[:, i:i + 1]),
                    deps=[td])
                if kind == "w13":
                    p1["tok"][r] = ta
                else:
                    p1["w2tok"] = ta
                p1["i"] = i + 1
                if i + 1 == n_p1_w1:
                    p1["w1_done"] = ta
                if i + 1 == n_p1_w13:
                    p1["w13_done"] = ta
                if i + 1 == n_p1_all:
                    p1["all_done"] = ta

        t_x = [B.g("sp", lambda e, ct=ct: e.dma_start(xt(ct), xsh[128 * ct:128 * (ct + 1), :]),
                   dma=True) for ct in range(NCT)]

        st1_banks = {0: (PS[0], PS[3]), 1: (PS[1], PS[4]), 2: (PS[2], PS[5])}
        t_st1 = {}
        rotor_rd = {}
        for ci, (c0, c1) in enumerate(CH_X):
            n = c1 - c0
            for ct in range(NCT):
                r = ct % 2
                def f_c(e, ct=ct, r=r, c0=c0, c1=c1, n=n):
                    e.copy(cb1[r][:, :n], xt(ct)[:, c0:c1])
                    return e.activation(sb1[r][:, :n], xt(ct)[:, c0:c1], AF.Square)
                tc_ = B.g("act", f_c, deps=[t_x[ct], rotor_rd.get(r)])
                bA, bB = st1_banks[ci]
                def f_s(e, ct=ct, r=r, n=n, bA=bA, bB=bB):
                    e.matmul(bA[0:1, 0:n], G4[:, 0:1], cb1[r][:, :n],
                             start=(ct == 0), stop=(ct == NCT - 1))
                    return e.matmul(bB[0:1, 0:n], G4[:, 0:1], sb1[r][:, :n],
                                    start=(ct == 0), stop=(ct == NCT - 1))
                t_st1[(ci, ct)] = B.g("pe", f_s, deps=[tc_, t_consta])
                rotor_rd[r] = t_st1[(ci, ct)]
            emit_p1(2)

        def ln_rows(bA, bB, n_chunk, dep, eps_ap, bc_banks, war_bc):
            toks = []
            prev_bc = war_bc
            for sc in range(0, n_chunk, 256):
                nn = min(256, n_chunk - sc)
                def f_rv(e, sc=sc, nn=nn, bA=bA, bB=bB):
                    e.tensor_scalar(row("mu", nn), bA[0:1, sc:sc + nn], 1.0 / D, None,
                                    ALU.mult)
                    e.tensor_scalar(row("v1", nn), bB[0:1, sc:sc + nn], 1.0 / D, None,
                                    ALU.mult)
                    e.tensor_tensor(row("v2", nn), row("mu", nn), row("mu", nn), ALU.mult)
                    return e.tensor_tensor(row("v1", nn), row("v1", nn), row("v2", nn),
                                           ALU.subtract)
                trv = B.g("dve", f_rv, deps=[dep, prev_bc])
                te_ = B.g("act", lambda e, nn=nn, eps_ap=eps_ap: (
                    e.activation(row("v2", nn), row("v1", nn), AF.Ln, bias=eps_ap),
                    e.activation(row("rstd", nn), row("v2", nn), AF.Exp, scale=-0.5))[-1],
                    deps=[trv])
                tb_ = B.g("dve", lambda e, nn=nn: e.scalar_tensor_tensor(
                    row("brow", nn), row("mu", nn), -1.0, row("rstd", nn), ALU.mult, ALU.mult),
                    deps=[te_])
                def f_bc(e, sc=sc, nn=nn):
                    e.matmul(bc_banks[0][:, sc:sc + nn], ONE_R[:], row("rstd", nn),
                             start=True, stop=True)
                    return e.matmul(bc_banks[1][:, sc:sc + nn], ONE_R[:], row("brow", nn),
                                    start=True, stop=True)
                tbc = B.g("pe", f_bc, deps=[tb_])
                prev_bc = tbc
                toks.append(tbc)
            return toks[-1]

        t_bce1 = {}
        prev_bce = None
        for ci, (c0, c1) in enumerate(CH_X):
            n = c1 - c0
            bA, bB = st1_banks[ci]
            tbc = ln_rows(bA, bB, n, t_st1[(ci, NCT - 1)], R_S["eps"], (PS[6], PS[7]), prev_bce)
            def f_bce(e, ci=ci, n=n):
                e.tensor_copy(aCh[ci][:, :n], PS[6][:, 0:n])
                return e.tensor_copy(bCh[ci][:, :n], PS[7][:, 0:n])
            t_bce1[ci] = B.g("dve", f_bce, deps=[tbc])
            prev_bce = t_bce1[ci]

        t_mix = {}
        t_a1_prev = {"pool": None, "dve": None}
        for ct in range(NCT):
            r = ct % 2
            x1r = X1R[:, 1090 * r:1090 * r + 1089]
            eng1 = "pool" if ct % 2 == 0 else "dve"
            sbase = 19600 + 512 * (ct % 2)
            def f_a1(e, ct=ct, x1r=x1r, sbase=sbase):
                sc_ap = cv(A1, sbase, 512)
                for ci, (c0, c1) in enumerate(CH_X):
                    n = c1 - c0
                    e.tensor_tensor(sc_ap[:, :n], xt(ct)[:, c0:c1], aCh[ci][:, :n], ALU.mult)
                    last = e.tensor_tensor(x1r[:, c0:c1], sc_ap[:, :n], bCh[ci][:, :n],
                                           ALU.add)
                return last
            t_a1 = B.g(eng1, f_a1, deps=[t_bce1[2], t_x[ct], t_mix.get(ct - 2),
                                         t_a1_prev[eng1]])
            t_a1_prev[eng1] = t_a1
            def f_mx(e, ct=ct, x1r=x1r):
                dd = d_rot[ct % 2]
                e.tensor_tensor(dd[:], x1r[:, 0:1088], x1r[:, 1:1089], ALU.subtract)
                e.scalar_tensor_tensor(xmk(ct)[:], dd[:], tmk_c(ct), x1r[:, 1:1089],
                                       ALU.mult, ALU.add)
                return e.scalar_tensor_tensor(xmr(ct)[:], dd[:], tmr_c(ct), x1r[:, 1:1089],
                                              ALU.mult, ALU.add)
            t_mix[ct] = B.g("dve", f_mx, deps=[t_a1, t_mix.get(ct - 2)])
            if ct % 2 == 1:
                emit_p1(2)
        t_mix_all = t_mix[NCT - 1]

        kb = {0: PS[0], 1: PS[1], 2: PS[2]}
        rb = {0: PS[3], 1: PS[4], 2: PS[5]}
        t_wk, t_wr, t_kmm, t_rmm, t_relu, t_sblk = {}, {}, {}, {}, {}, {}

        def emit_kmm(ht):
            r = ht % 2
            def f_k(e, r=r):
                for ci, (cs0, cs1) in enumerate(CH_S):
                    ns = cs1 - cs0
                    for ct in range(NCT):
                        last = e.matmul(kb[ci][:, 0:ns], wkb(r)[:, 128 * ct:128 * (ct + 1)],
                                        xmk(ct)[:, cs0:cs1], start=(ct == 0),
                                        stop=(ct == NCT - 1))
                return last
            t_kmm[ht] = B.g("pe", f_k, deps=[t_wk[ht], t_mix_all, t_relu.get(ht - 1)])

        def emit_rmm(ht):
            def f_r(e):
                for ci, (cs0, cs1) in enumerate(CH_S):
                    ns = cs1 - cs0
                    for ct in range(NCT):
                        last = e.matmul(rb[ci][:, 0:ns], wrb[:, 128 * ct:128 * (ct + 1)],
                                        xmr(ct)[:, cs0:cs1], start=(ct == 0),
                                        stop=(ct == NCT - 1))
                return last
            t_rmm[ht] = B.g("pe", f_r, deps=[t_wr[ht], t_mix_all, t_sblk.get(ht - 1)])
            r = ht % 2
            t_relu[ht] = B.g("act", lambda e, r=r: (
                e.activation(a_sb[r][:, 0:512], kb[0][:, 0:512], AF.Relu),
                e.activation(a_sb[r][:, 512:1024], kb[1][:, 0:512], AF.Relu),
                e.activation(a_sb[r][:, 1024:1088], kb[2][:, 0:64], AF.Relu))[-1],
                deps=[t_kmm[ht], t_sblk.get(ht - 2)])
            def f_sb(e, ht=ht, r=r):
                e.tensor_tensor(a_sb[r][:, 0:64], a_sb[r][:, 0:64], mask_t, ALU.mult)
                e.tensor_tensor_scan(state_t[:], WDEC[:, ht:ht + 1].broadcast_to([128, TS_]),
                                     a_sb[r][:], 0.0, ALU.mult, ALU.add)
                e.tensor_tensor(sr_t(ht)[:, 0:448], state_t[:, 64:512], rb[0][:, 64:512],
                                ALU.mult)
                e.tensor_tensor(sr_t(ht)[:, 448:960], state_t[:, 512:1024], rb[1][:, 0:512],
                                ALU.mult)
                return e.tensor_tensor(sr_t(ht)[:, 960:1024], state_t[:, 1024:1088],
                                       rb[2][:, 0:64], ALU.mult)
            t_sblk[ht] = B.g("dve", f_sb, deps=[t_relu[ht], t_wdec, t_rmm[ht],
                                                t_sblk.get(ht - 1)])

        for ht in range(NHT):
            r = ht % 2
            t_wk[ht] = B.g("pool", lambda e, ht=ht, r=r: e.dma_start(wkb(r), wk_h[ht]),
                           deps=[t_kmm.get(ht - 2)], dma=True)
            if ht >= 1:
                emit_rmm(ht - 1)
            t_wr[ht] = B.g("pool", lambda e, ht=ht: e.dma_start(wrb, wr_h[ht]),
                           deps=[t_rmm.get(ht - 1)], dma=True)
            emit_kmm(ht)
            emit_p1(3)
        emit_rmm(NHT - 1)
        t_outr_all = t_sblk[NHT - 1]
        t_kmm_all = t_kmm[NHT - 1]
        t_rmm_all = t_rmm[NHT - 1]

        if DEBUG:
            for ht in range(NHT):
                B.g("sp", lambda e, ht=ht: e.dma_start(
                    dbg["d_sr"][128 * ht:128 * (ht + 1), :], sr_t(ht)),
                    deps=[t_outr_all], dma=True)

        ypb = {0: PS[0], 1: PS[1]}
        t_wv, t_ymm, t_yevac = {}, {}, {}
        for ct in range(NCT):
            r = ct % 2
            t_wv[ct] = B.g("pool", lambda e, ct=ct, r=r: e.dma_start(wvb(r), wv_h[ct]),
                           deps=[t_ymm.get((ct - 2, 1)), t_kmm_all, t_rmm_all], dma=True)
            for tc in range(2):
                def f_y(e, r=r, tc=tc):
                    for ht in range(NHT):
                        last = e.matmul(ypb[tc][:, 0:512], wvb(r)[:, 128 * ht:128 * (ht + 1)],
                                        sr_t(ht)[:, 512 * tc:512 * (tc + 1)],
                                        start=(ht == 0), stop=(ht == NHT - 1))
                    return last
                t_ymm[(ct, tc)] = B.g("pe", f_y, deps=[t_wv[ct], t_outr_all,
                                                       t_yevac.get((ct - 1, tc))])
                t_yevac[(ct, tc)] = B.g("act", lambda e, ct=ct, tc=tc: e.copy(
                    y_t(ct)[:, 512 * tc:512 * (tc + 1)], ypb[tc][:, 0:512]),
                    deps=[t_ymm[(ct, tc)]])
            emit_p1(4)
        t_ymm_all = t_ymm[(NCT - 1, 1)]
        while p1["i"] < n_p1_w13:
            emit_p1(4)

        if DEBUG:
            for ct in range(NCT):
                B.g("sp", lambda e, ct=ct: e.dma_start(
                    dbg["d_y"][128 * ct:128 * (ct + 1), :], y_t(ct)),
                    deps=[t_yevac[(ct, 0)], t_yevac[(ct, 1)]], dma=True)

        t_x2w = {}
        prev_tc_done = None
        t_ap4 = {}
        for tc in range(2):
            t_st23 = {}
            for ct in range(NCT):
                r = ct % 2
                def f_yb(e, ct=ct, tc=tc, r=r):
                    e.copy(ybf_r[r][:], y_t(ct)[:, 512 * tc:512 * (tc + 1)])
                    return e.activation(ysq_r[r][:], y_t(ct)[:, 512 * tc:512 * (tc + 1)],
                                        AF.Square)
                t_yb = B.g("act", f_yb, deps=[t_yevac[(ct, tc)], t_st23.get(ct - 2),
                                              prev_tc_done])
                def f_st(e, ct=ct, r=r):
                    e.matmul(PS[2][0:1, 0:512], G4[:, 4 * ct:4 * ct + 1], ybf_r[r][:],
                             start=(ct == 0), stop=(ct == NCT - 1))
                    e.matmul(PS[3][0:1, 0:512], G4[:, 4 * ct + 1:4 * ct + 2], ybf_r[r][:],
                             start=(ct == 0), stop=(ct == NCT - 1))
                    e.matmul(PS[4][0:1, 0:512], G4[:, 4 * ct + 2:4 * ct + 3], ybf_r[r][:],
                             start=(ct == 0), stop=(ct == NCT - 1))
                    e.matmul(PS[5][0:1, 0:512], G4[:, 4 * ct + 3:4 * ct + 4], ybf_r[r][:],
                             start=(ct == 0), stop=(ct == NCT - 1))
                    e.matmul(PS[6][0:1, 0:512], G2[:, 2 * ct:2 * ct + 1], ysq_r[r][:],
                             start=(ct == 0), stop=(ct == NCT - 1))
                    return e.matmul(PS[7][0:1, 0:512], G2[:, 2 * ct + 1:2 * ct + 2], ysq_r[r][:],
                                    start=(ct == 0), stop=(ct == NCT - 1))
                t_st23[ct] = B.g("pe", f_st, deps=[t_yb])
            sub_toks = []
            prev_bc = None
            for sc in range(0, 512, 256):
                nn = 256
                def f_rv1(e, sc=sc, nn=nn):
                    e.tensor_scalar(row("mu", nn), PS[2][0:1, sc:sc + nn], 1.0 / D, None,
                                    ALU.mult)
                    e.tensor_scalar(row("v1", nn), PS[6][0:1, sc:sc + nn], 1.0 / D, None,
                                    ALU.mult)
                    e.tensor_tensor(row("v2", nn), row("mu", nn), row("mu", nn), ALU.mult)
                    return e.tensor_tensor(row("v1", nn), row("v1", nn), row("v2", nn),
                                           ALU.subtract)
                trv1 = B.g("dve", f_rv1, deps=[t_st23[NCT - 1], t_sred, prev_bc])
                te1 = B.g("act", lambda e, nn=nn: (
                    e.activation(row("v2", nn), row("v1", nn), AF.Ln, bias=R_S["eps"]),
                    e.activation(row("ry", nn), row("v2", nn), AF.Exp, scale=-0.5))[-1],
                    deps=[trv1])
                mu = row("mu", nn); ry = row("ry", nn)
                t1 = row("v2", nn); t2 = row("brow", nn); t3 = row("rstd", nn)
                ps3 = PS[3][0:1, sc:sc + nn]; ps4 = PS[4][0:1, sc:sc + nn]
                ps5 = PS[5][0:1, sc:sc + nn]; ps7 = PS[7][0:1, sc:sc + nn]
                def f_rv2(e, mu=mu, ry=ry, t1=t1, t2=t2, t3=t3, ps3=ps3, ps4=ps4,
                          ps5=ps5, ps7=ps7):
                    e.scalar_tensor_tensor(t1, mu, R_S["S1"], ps3, ALU.mult, ALU.subtract)
                    e.scalar_tensor_tensor(t1, t1, -1.0, ry, ALU.mult, ALU.mult)
                    e.tensor_scalar(t2, t1, R_S["S2"], 1.0 / D, ALU.add, ALU.mult)
                    e.tensor_scalar(t3, ps4, 2.0, None, ALU.mult)
                    e.scalar_tensor_tensor(t3, mu, R_S["S3"], t3, ALU.mult, ALU.subtract)
                    e.tensor_tensor(t3, t3, mu, ALU.mult)
                    e.tensor_tensor(t3, ps7, t3, ALU.add)
                    e.tensor_tensor(t1, ry, ry, ALU.mult)
                    e.tensor_tensor(t3, t3, t1, ALU.mult)
                    e.scalar_tensor_tensor(t1, mu, R_S["S4"], ps5, ALU.mult, ALU.subtract)
                    e.scalar_tensor_tensor(t1, t1, -1.0, ry, ALU.mult, ALU.mult)
                    e.scalar_tensor_tensor(t3, t1, 2.0, t3, ALU.mult, ALU.add)
                    e.tensor_scalar(t3, t3, R_S["S5"], 1.0 / D, ALU.add, ALU.mult)
                    e.tensor_tensor(t1, t2, t2, ALU.mult)
                    return e.tensor_tensor(t3, t3, t1, ALU.subtract)
                trv2 = B.g("dve", f_rv2, deps=[te1])
                te2 = B.g("act", lambda e, nn=nn: (
                    e.activation(row("rstd", nn), row("rstd", nn), AF.Ln, bias=R_S["eps"]),
                    e.activation(row("rstd", nn), row("rstd", nn), AF.Exp, scale=-0.5))[-1],
                    deps=[trv2])
                def f_rv3(e, nn=nn):
                    e.tensor_tensor(row("v1", nn), row("ry", nn), row("rstd", nn), ALU.mult)
                    e.scalar_tensor_tensor(row("v2", nn), row("brow", nn), -1.0,
                                           row("rstd", nn), ALU.mult, ALU.mult)
                    return e.scalar_tensor_tensor(row("mu", nn), row("mu", nn), -1.0,
                                                  row("v1", nn), ALU.mult, ALU.mult)
                trv3 = B.g("dve", f_rv3, deps=[te2])
                def f_bce_cast(e, sc=sc, nn=nn):
                    e.copy(RWB3[0:1, sc:sc + nn], row("mu", nn))
                    e.copy(RWB[:, sc:sc + nn], row("rstd", nn))
                    return e.copy(RWB[:, 512 + sc:512 + sc + nn], row("v2", nn))
                tcast = B.g("act", f_bce_cast, deps=[trv3])
                prev_bc = B.g("pe", lambda e, sc=sc, nn=nn: e.matmul(
                    PS[2][:, sc:sc + nn], ONE_R[:], row("v1", nn), start=True, stop=True),
                    deps=[trv3, tcast])
                sub_toks.append((prev_bc, tcast))
            sub_pe = [t for (t, _) in sub_toks]
            sub_cast = [t for (_, t) in sub_toks]
            t_rep32 = B.g("sp", lambda e: e.dma_start(RWB3[1:2, :], RWB[:, 0:512]),
                          deps=[sub_cast[-1]], dma=True)
            t_rep64 = B.g("sp", lambda e: e.dma_start(RWB3[2:3, :], RWB[:, 512:1024]),
                          deps=[sub_cast[-1]], dma=True)
            t_bce23 = B.g("dve", lambda e: e.tensor_copy(bcA[:], PS[2][:, 0:512]),
                          deps=sub_pe)

            t_ap23 = {}
            t_wmm = {}
            t_st4 = {}
            wb_bank = lambda ct: PS[3 + (ct % 3)]
            for ct in range(NCT):
                r = ct % 2
                def f_w(e, ct=ct):
                    return e.matmul(wb_bank(ct)[:, 0:512], GROWB[0:3, 128 * ct:128 * (ct + 1)],
                                    RWB3[0:3, 0:512], start=True, stop=True)
                t_wmm[ct] = B.g("pe", f_w, deps=[t_rep32, t_rep64, sub_cast[-1]] +
                                t_growb_all + [t_ap23.get(ct - 3)])
                t_xres = B.g("sp", lambda e, ct=ct, tc=tc, r=r: e.dma_start(
                    xres3[r], xsh[128 * ct:128 * (ct + 1), 65 + 512 * tc:65 + 512 * (tc + 1)]),
                    deps=[t_ap23.get(ct - 2), t_bce23], dma=True)
                def f_ap(e, ct=ct, tc=tc, r=r):
                    yv = y_t(ct)[:, 512 * tc:512 * (tc + 1)]
                    e.tensor_tensor(aps3[:], yv, bcA[:], ALU.mult)
                    e.tensor_tensor(aps1[:], wb_bank(ct)[:, 0:512], xres3[r], ALU.add)
                    return e.scalar_tensor_tensor(yv, aps3[:], gam_c(ct), aps1[:],
                                                  ALU.mult, ALU.add)
                t_ap23[ct] = B.g("dve", f_ap, deps=[t_bce23, t_xres, t_wmm[ct],
                                                    t_ap23.get(ct - 1)])
                t_x2w[(ct, tc)] = B.g("sp", lambda e, ct=ct, tc=tc: e.dma_start(
                    yout[128 * ct:128 * (ct + 1), 512 * tc:512 * (tc + 1)],
                    y_t(ct)[:, 512 * tc:512 * (tc + 1)]), deps=[t_ap23[ct]], dma=True)
                def f_x2b(e, ct=ct, tc=tc, r=r):
                    e.copy(x2bf_r[r][:], y_t(ct)[:, 512 * tc:512 * (tc + 1)])
                    return e.activation(x2sq_r[r][:], y_t(ct)[:, 512 * tc:512 * (tc + 1)],
                                        AF.Square)
                t_xb = B.g("act", f_x2b, deps=[t_ap23[ct], t_st4.get(ct - 2)])
                def f_s4(e, ct=ct, r=r):
                    e.matmul(PS[6][0:1, 0:512], G4[:, 0:1], x2bf_r[r][:],
                             start=(ct == 0), stop=(ct == NCT - 1))
                    return e.matmul(PS[7][0:1, 0:512], G4[:, 0:1], x2sq_r[r][:],
                                    start=(ct == 0), stop=(ct == NCT - 1))
                t_st4[ct] = B.g("pe", f_s4, deps=[t_xb])
            tbc4 = ln_rows(PS[6], PS[7], 512, t_st4[NCT - 1], R_S["eps"], (PS[2], PS[3]),
                           t_bce23)
            def f_bce4(e):
                e.tensor_copy(bcA[:], PS[2][:, 0:512])
                return e.tensor_copy(bcB[:], PS[3][:, 0:512])
            t_bce4 = B.g("dve", f_bce4, deps=[tbc4])
            for ct in range(NCT):
                def f_a4(e, ct=ct, tc=tc):
                    e.tensor_tensor(aps1[:], y_t(ct)[:, 512 * tc:512 * (tc + 1)], bcA[:],
                                    ALU.mult)
                    return e.tensor_tensor(x2n_t(ct)[:, 512 * tc:512 * (tc + 1)], aps1[:],
                                           bcB[:], ALU.add)
                t_ap4[(tc, ct)] = B.g("dve", f_a4, deps=[t_bce4, p1["w13_done"],
                                                         t_x2w[(ct, tc)],
                                                         t_ap4.get((tc, ct - 1))])
            prev_tc_done = t_ap4[(tc, NCT - 1)]
        t_ap4_all = prev_tc_done

        if DEBUG:
            for ct in range(NCT):
                B.g("sp", lambda e, ct=ct: e.dma_start(
                    dbg["d_x2n"][128 * ct:128 * (ct + 1), :], x2n_t(ct)),
                    deps=[t_ap4_all], dma=True)
                B.g("sp", lambda e, ct=ct: e.dma_start(
                    dbg["d_x2"][128 * ct:128 * (ct + 1), :], y_t(ct)),
                    deps=[t_ap4_all], dma=True)

        def s_finalize(acc_lo, acc_hi, dep, thr_ap, nthr_ap, s_slot, t1, t2, acc_col,
                       extra=None):
            tred = B.g("dve", lambda e: e.tensor_reduce(
                ACC[:, acc_col:acc_col + 1], ACC[:, acc_lo:acc_hi], AX.X, ALU.add),
                deps=[dep, extra])
            tmm = B.g("pe", lambda e: e.matmul(PS[6][0:1, 0:1], ONE_C[:],
                                               ACC[:, acc_col:acc_col + 1],
                                               start=True, stop=True), deps=[tred])
            ts_ = B.g("dve", lambda e: e.tensor_scalar(
                s_slot, PS[6][0:1, 0:1], 1.0 / (F_REAL * D), None, ALU.mult), deps=[tmm],
                settle=1)
            tt1 = B.g("dve", lambda e: e.tensor_scalar(t1, s_slot, 0.5, None, ALU.mult),
                      deps=[ts_], settle=1)
            tt2 = B.g("dve", lambda e: e.tensor_scalar(t2, t1, -1.0, None, ALU.mult),
                      deps=[tt1])
            def f_bc(e):
                e.matmul(PS[6][:, 0:1], ONE_R[:], t1, start=True, stop=True)
                return e.matmul(PS[7][:, 0:1], ONE_R[:], t2, start=True, stop=True)
            tbc = B.g("pe", f_bc, deps=[tt2])
            def f_e(e):
                e.tensor_copy(thr_ap, PS[6][:, 0:1])
                return e.tensor_copy(nthr_ap, PS[7][:, 0:1])
            return B.g("dve", f_e, deps=[tbc])

        t_thr1 = s_finalize(0, n_p1_w1, p1["w1_done"], thr1c, nthr1c, R_S["s1"],
                            R_S["t1a"], R_S["t2a"], 192)
        t_thr3 = s_finalize(n_p1_w1, n_p1_w13, p1["w13_done"], thr3c, nthr3c, R_S["s3"],
                            R_S["t1b"], R_S["t2b"], 193)
        t_s1bc = B.g("pe", lambda e: e.matmul(PS[6][:, 0:1], ONE_R[:], R_S["s1"],
                                              start=True, stop=True), deps=[t_thr1, t_thr3])
        t_s1col = B.g("dve", lambda e: e.tensor_copy(s1col, PS[6][:, 0:1]), deps=[t_s1bc])

        h1b = {0: (PS[0], PS[1]), 1: (PS[4], PS[5])}
        h3b = {0: (PS[2], PS[3]), 1: (PS[6], PS[7])}
        t_q1, t_q3, t_h1, t_h3, t_silu, t_g = {}, {}, {}, {}, {}, {}
        qst = {"last_tb1": None, "last_tb3": None,
               "stg1": {}, "stg3": {}}

        def emit_q(ft, which):
            r = ft % 2
            if which == 1:
                wh, stg, qt, qbv, thr, nthr, tthr = (w1_h, wstg1, qtmp1, q1b_t, thr1c, nthr1c,
                                                     t_thr1)
                stg_used, last_key, tq, war_qb = qst["stg1"], "last_tb1", t_q1, t_h1.get((ft - 2, 1))
            else:
                wh, stg, qt, qbv, thr, nthr, tthr = (w3_h, wstg3, qtmp3, q3b_t, thr3c, nthr3c,
                                                     t_thr3)
                stg_used, last_key, tq, war_qb = qst["stg3"], "last_tb3", t_q3, t_h3.get((ft - 2, 1))
            tb = None
            for h_ in range(2):
                td = B.g("sp", lambda e, ft=ft, h_=h_, wh=wh, stg=stg: e.dma_start(
                    stg[h_], wh[ft, :, 1024 * h_:1024 * (h_ + 1)]),
                    deps=[stg_used.get(h_), t_ymm_all], dma=True)
                ta = B.g("dve", lambda e, h_=h_, stg=stg, qt=qt, nthr=nthr:
                         e.tensor_single_scalar(qt, stg[h_], nthr, ALU.is_lt),
                         deps=[td, tthr, qst[last_key], war_qb])
                tb = B.g("dve", lambda e, r=r, h_=h_, stg=stg, qt=qt, qbv=qbv, thr=thr:
                         e.scalar_tensor_tensor(qbv(r)[:, 1024 * h_:1024 * (h_ + 1)], stg[h_],
                                                thr, qt, ALU.is_gt, ALU.subtract),
                         deps=[ta])
                stg_used[h_] = tb
                qst[last_key] = tb
            tq[ft] = tb

        emit_q(0, 1)
        emit_q(0, 3)
        t_ap4_tc = {0: t_ap4[(0, NCT - 1)], 1: t_ap4[(1, NCT - 1)]}
        for ft in range(NFT):
            r = ft % 2
            if ft + 1 < NFT:
                emit_q(ft + 1, 1)
                emit_q(ft + 1, 3)
            bb1 = h1b[r]
            bb3 = h3b[r]
            for tc in range(2):
                def f_h1(e, r=r, bb1=bb1, tc=tc):
                    for ct in range(NCT):
                        last = e.matmul(bb1[tc][:, 0:512], q1b_t(r)[:, 128 * ct:128 * (ct + 1)],
                                        x2n_t(ct)[:, 512 * tc:512 * (tc + 1)],
                                        start=(ct == 0), stop=(ct == NCT - 1))
                    return last
                t_h1[(ft, tc)] = B.g("pe", f_h1, deps=[t_q1[ft], t_ap4_tc[tc],
                                                       t_silu.get((ft - 2, tc))])
                def f_h3(e, r=r, bb3=bb3, tc=tc):
                    for ct in range(NCT):
                        last = e.matmul(bb3[tc][:, 0:512], q3b_t(r)[:, 128 * ct:128 * (ct + 1)],
                                        x2n_t(ct)[:, 512 * tc:512 * (tc + 1)],
                                        start=(ct == 0), stop=(ct == NCT - 1))
                    return last
                t_h3[(ft, tc)] = B.g("pe", f_h3, deps=[t_q3[ft], t_ap4_tc[tc],
                                                       t_g.get((ft - 2, tc)),
                                                       t_s1col if ft == 1 else None])
                t_silu[(ft, tc)] = B.g("act", lambda e, r=r, bb1=bb1, tc=tc: e.activation(
                    silu_t(r)[:, 512 * tc:512 * (tc + 1)], bb1[tc][:, 0:512], AF.Silu,
                    scale=s1col),
                    deps=[t_h1[(ft, tc)], t_s1col, t_g.get((ft - 2, tc))])
                t_g[(ft, tc)] = B.g("dve", lambda e, ft=ft, r=r, bb3=bb3, tc=tc: e.tensor_tensor(
                    g_t(ft)[:, 512 * tc:512 * (tc + 1)], silu_t(r)[:, 512 * tc:512 * (tc + 1)],
                    bb3[tc][:, 0:512], ALU.mult),
                    deps=[t_silu[(ft, tc)], t_h3[(ft, tc)]])
            emit_p1(2, allow_w2=True, extra_dep=t_ymm_all)
        t_g_all = t_g[(NFT - 1, 1)]
        t_g_tc = {0: t_g[(NFT - 1, 0)], 1: t_g[(NFT - 1, 1)]}

        if DEBUG:
            for ft in range(NFT):
                B.g("sp", lambda e, ft=ft: e.dma_start(
                    dbg["d_g"][128 * ft:128 * (ft + 1), :], g_t(ft)), deps=[t_g_all], dma=True)

        while p1["i"] < n_p1_all:
            emit_p1(4, allow_w2=True, extra_dep=t_ymm_all)
        t_thr2 = s_finalize(n_p1_w13, n_p1_all, p1["all_done"], thr2c, nthr2c, R_S["s2"],
                            R_S["t1c"], R_S["t2c"], 194, extra=t_g_all)

        if DEBUG:
            B.g("sp", lambda e: e.dma_start(dbg["d_acc"], ACC[:]),
                deps=[t_thr2], dma=True)
            B.g("sp", lambda e: e.dma_start(dbg["d_scl"], SCL[:]),
                deps=[t_thr2, t_s1col], dma=True)
            B.g("sp", lambda e: e.dma_start(dbg["d_rw"], RW[:]),
                deps=[t_thr2], dma=True)

        te5a = B.g("dve", lambda e: e.tensor_tensor(R_S["t1e"], R_S["s2"], R_S["s3"],
                                                    ALU.mult), deps=[t_thr2, t_thr3],
                   settle=1)
        te5b = B.g("dve", lambda e: e.tensor_tensor(R_S["t1e"], R_S["t1e"], R_S["t1e"],
                                                    ALU.mult), deps=[te5a], settle=1)
        te5c = B.g("dve", lambda e: e.reciprocal(R_S["t2e"], R_S["t1e"]), deps=[te5b],
                   settle=1)
        t_eps5 = B.g("dve", lambda e: e.tensor_scalar(R_S["eps5"], R_S["t2e"], EPS, None,
                                                      ALU.mult), deps=[te5c])

        y2b = {0: PS[0], 1: PS[1]}
        t_q2, t_y2, t_y2bf, t_st5 = {}, {}, {}, {}
        q2st = {"last_tb": None, "stg": {}}

        def emit_q2(ct):
            r = ct % 2
            tb = None
            for h_ in range(6):
                lo = 1024 * h_
                hi = min(lo + 1024, 5504)
                n = hi - lo
                rr = h_ % 2
                td = B.g("sp", lambda e, ct=ct, rr=rr, lo=lo, hi=hi, n=n: e.dma_start(
                    wstg2[rr][:, 0:n], w2_h[ct, :, lo:hi]),
                    deps=[q2st["stg"].get(rr), t_g_all], dma=True)
                ta = B.g("dve", lambda e, rr=rr, n=n: e.tensor_single_scalar(
                    qtmp2[:, 0:n], wstg2[rr][:, 0:n], nthr2c, ALU.is_lt),
                    deps=[td, t_thr2, q2st["last_tb"], t_y2.get((ct - 2, 1))])
                tb = B.g("dve", lambda e, r=r, rr=rr, lo=lo, hi=hi, n=n: e.scalar_tensor_tensor(
                    q2b_t(r)[:, lo:hi], wstg2[rr][:, 0:n], thr2c, qtmp2[:, 0:n],
                    ALU.is_gt, ALU.subtract), deps=[ta])
                q2st["stg"][rr] = tb
                q2st["last_tb"] = tb
            t_q2[ct] = tb

        emit_q2(0)
        for ct in range(NCT):
            r = ct % 2
            if ct + 1 < NCT:
                emit_q2(ct + 1)
            for tc in range(2):
                def f_y2(e, r=r, tc=tc):
                    for ft in range(NFT):
                        last = e.matmul(y2b[tc][:, 0:512], q2b_t(r)[:, 128 * ft:128 * (ft + 1)],
                                        g_t(ft)[:, 512 * tc:512 * (tc + 1)],
                                        start=(ft == 0), stop=(ft == NFT - 1))
                    return last
                t_y2[(ct, tc)] = B.g("pe", f_y2, deps=[t_q2[ct], t_g_tc[tc],
                                                       t_y2bf.get((ct - 1, tc))])
                def f_y2b(e, ct=ct, tc=tc):
                    e.copy(y2bf_t(ct)[:, 512 * tc:512 * (tc + 1)], y2b[tc][:, 0:512])
                    return e.activation(y2sq_r[tc][:], y2b[tc][:, 0:512], AF.Square)
                t_y2bf[(ct, tc)] = B.g("act", f_y2b, deps=[t_y2[(ct, tc)],
                                                           t_st5.get(ct - 1)])
            def f_s5(e, ct=ct):
                e.matmul(PS[2][0:1, 0:512], G4[:, 0:1], y2bf_t(ct)[:, 0:512],
                         start=(ct == 0), stop=(ct == NCT - 1))
                e.matmul(PS[3][0:1, 0:512], G4[:, 0:1], y2bf_t(ct)[:, 512:1024],
                         start=(ct == 0), stop=(ct == NCT - 1))
                e.matmul(PS[4][0:1, 0:512], G4[:, 0:1], y2sq_r[0][:],
                         start=(ct == 0), stop=(ct == NCT - 1))
                return e.matmul(PS[5][0:1, 0:512], G4[:, 0:1], y2sq_r[1][:],
                                start=(ct == 0), stop=(ct == NCT - 1))
            t_st5[ct] = B.g("pe", f_s5, deps=[t_y2bf[(ct, 0)], t_y2bf[(ct, 1)]])

        t_ap5 = {}
        xre_user = {}
        out_user = {}
        prev5 = None
        for tc in range(2):
            pSy = PS[2] if tc == 0 else PS[3]
            pS2 = PS[4] if tc == 0 else PS[5]
            tbc5 = ln_rows(pSy, pS2, 512, t_st5[NCT - 1], R_S["eps5"], (PS[6], PS[7]), prev5)
            def f_bce5(e):
                e.tensor_copy(bcA[:], PS[6][:, 0:512])
                return e.tensor_copy(bcB[:], PS[7][:, 0:512])
            t_bce5 = B.g("dve", f_bce5, deps=[tbc5, t_eps5])
            for ct in range(NCT):
                r3 = ct % 3
                r2 = ct % 2
                t_xre = B.g("sp", lambda e, ct=ct, tc=tc, r3=r3: e.dma_start(
                    xre6[r3], yout[128 * ct:128 * (ct + 1), 512 * tc:512 * (tc + 1)]),
                    deps=[t_x2w[(ct, tc)], xre_user.get(r3), t_y2bf[(NCT - 1, 1)]], dma=True)
                def f_a5(e, ct=ct, tc=tc, r3=r3, r2=r2):
                    e.tensor_tensor(aps6[:], y2bf_t(ct)[:, 512 * tc:512 * (tc + 1)], bcA[:],
                                    ALU.mult)
                    e.tensor_tensor(aps6[:], aps6[:], bcB[:], ALU.add)
                    return e.tensor_tensor(out6[r2], aps6[:], xre6[r3], ALU.add)
                t_ap5[(tc, ct)] = B.g("dve", f_a5, deps=[t_bce5, t_xre, out_user.get(r2),
                                                         t_ap5.get((tc, ct - 1))])
                xre_user[r3] = t_ap5[(tc, ct)]
                out_user[r2] = B.g("sp", lambda e, ct=ct, tc=tc, r2=r2: e.dma_start(
                    yout[128 * ct:128 * (ct + 1), 512 * tc:512 * (tc + 1)], out6[r2]),
                    deps=[t_ap5[(tc, ct)]], dma=True)
            prev5 = t_ap5[(tc, NCT - 1)]

        awaited = set()
        for e in ENGINES:
            for (fn, deps, sem, amt, settle, v) in B.groups[e]:
                for (s, dv) in deps:
                    if s != e:
                        awaited.add((s, dv))
        sp_finals = [(f"d_sp{ln}", B.counts.get(f"d_sp{ln}", 0)) for ln in range(N_LANES["sp"])]

        with nc.Block("main") as blk:
            def body_for(name):
                def body(eng):
                    waited = {}
                    for (fn, deps, sem, amt, settle, v) in B.groups[name]:
                        for (s, dv) in deps:
                            if waited.get(s, 0) < dv:
                                eng.wait_ge(sems[s], dv)
                                waited[s] = dv
                        last = fn(eng)
                        if amt == 16:
                            last.then_inc(sems[sem], amt)
                        elif (sem, v) in awaited or settle:
                            eng.drain(fusable=True).then_inc(sems[sem], amt)
                        else:
                            last.then_inc(sems[sem], amt)
                    if name == "sp":
                        for s, fv in sp_finals:
                            if fv:
                                eng.wait_ge(sems[s], fv)
                return body
            blk.sync(body_for("sp"))
            blk.tensor(body_for("pe"))
            blk.vector(body_for("dve"))
            blk.scalar(body_for("act"))
            blk.gpsimd(body_for("pool"))

    return nc


_PROG_CACHE = {}


def _prep_host(inputs):
    x = np.asarray(inputs["L0_attn_x"], np.float32)
    Wk = np.asarray(inputs["L0_attn_Wk"], np.float32)
    Wr = np.asarray(inputs["L0_attn_Wr"], np.float32)
    Wv = np.asarray(inputs["L0_attn_Wv"], np.float32)
    tmk = np.asarray(inputs["L0_attn_time_mix_k"], np.float32)
    tmr = np.asarray(inputs["L0_attn_time_mix_r"], np.float32)
    dec = np.asarray(inputs["L0_attn_time_decay"], np.float32)
    gam = np.asarray(inputs["L0_attn_ln_gamma"], np.float32)
    bet = np.asarray(inputs["L0_attn_ln_beta"], np.float32)
    w1 = np.asarray(inputs["L0_ffn_w1"], np.float32)
    w3 = np.asarray(inputs["L0_ffn_w3"], np.float32)
    w2 = np.asarray(inputs["L0_ffn_w2"], np.float32)

    wk_img = np.ascontiguousarray(
        Wk.reshape(NHT, 128, NCT, 128).transpose(0, 3, 2, 1)).reshape(NHT, 128, 2048)
    wr_img = np.ascontiguousarray(
        Wr.reshape(NHT, 128, NCT, 128).transpose(0, 3, 2, 1)).reshape(NHT, 128, 2048)
    wv_img = np.ascontiguousarray(
        Wv.reshape(NCT, 128, NHT, 128).transpose(0, 3, 2, 1)).reshape(NCT, 128, 2816)
    w1p = np.zeros((FP, D), np.float32); w1p[:F_REAL] = w1
    w3p = np.zeros((FP, D), np.float32); w3p[:F_REAL] = w3
    w2p = np.zeros((D, FP), np.float32); w2p[:, :F_REAL] = w2
    w1_img = np.ascontiguousarray(
        w1p.reshape(NFT, 128, NCT, 128).transpose(0, 3, 2, 1)).reshape(NFT, 128, 2048)
    w3_img = np.ascontiguousarray(
        w3p.reshape(NFT, 128, NCT, 128).transpose(0, 3, 2, 1)).reshape(NFT, 128, 2048)
    w2_img = np.ascontiguousarray(
        w2p.reshape(NCT, 128, NFT, 128).transpose(0, 3, 2, 1)).reshape(NCT, 128, 5504)

    gbr = np.zeros((3, 2048), np.float32)
    gbr[0] = gam; gbr[1] = bet; gbr[2] = 1.0

    vec_base = np.zeros((128, 152), np.float32)
    vec_base[:, 0:16] = tmk.reshape(16, 128).T
    vec_base[:, 16:32] = tmr.reshape(16, 128).T
    vec_base[:, 32:48] = gam.reshape(16, 128).T
    vec_base[:, 48:64] = bet.reshape(16, 128).T
    vec_base[:, 64:86] = dec.reshape(22, 128).T

    in_maps = []
    for core in range(8):
        b, j = divmod(core, 4)
        if j == 0:
            rows = np.concatenate([np.repeat(x[b, 1:2], 65, 0), x[b, 0:1024]], 0)
            m = 0.0
        else:
            rows = x[b, j * 1024 - 65: j * 1024 + 1024]
            m = 1.0
        xshard = np.ascontiguousarray(rows.T)
        v = vec_base.copy()
        v[:, 86:150] = m
        in_maps.append({
            "xsh": xshard, "vecs": v,
            "wk_h": wk_img, "wr_h": wr_img, "wv_h": wv_img,
            "w1_h": w1_img, "w3_h": w3_img, "w2_h": w2_img, "gbrow": gbr,
        })
    return in_maps


def kernel(**inputs):
    in_maps = _prep_host(inputs)
    if "nc" not in _PROG_CACHE:
        _PROG_CACHE["nc"] = build_program()
    nc = _PROG_CACHE["nc"]
    res = run_bass_kernel_spmd(nc, in_maps, core_ids=list(range(8)))
    out = np.empty((2, 4096, 2048), np.float32)
    for core in range(8):
        b, j = divmod(core, 4)
        out[b, j * 1024:(j + 1) * 1024, :] = np.asarray(res.results[core]["yout"],
                                                        dtype=np.float32).T
    return out
